# version 05
# baseline (speedup 1.0000x reference)
import numpy as np
from contextlib import ExitStack

import concourse.bass as bass
import concourse.mybir as mybir
from concourse.bass_utils import run_bass_kernel_spmd

F32 = mybir.dt.float32
BF16 = mybir.dt.bfloat16
AX = mybir.AxisListType.X
AF = mybir.ActivationFunctionType
OP = mybir.AluOpType

B, L, D = 2, 4096, 2048
NCORE = 8
TOK = 1024
HALO = 64
EXT = TOK + 2 * HALO
NT_EXT = EXT // 128
NT_OWN = TOK // 128
DH = 64
NH = 32
NS = 16
NEG = -1.0e30

_DEBUG = False
_PHASES = 6


class Slots:

    def __init__(self, nc, es, name, n):
        self.sems = [es.enter_context(nc.semaphore(f"{name}{i}")) for i in range(n)]
        self.cnt = [0] * n

    def dma(self, q, out, in_, slot=0):
        ins = q.dma_start(out=out, in_=in_)
        ins.then_inc(self.sems[slot], 16)
        self.cnt[slot] += 16
        return (slot, self.cnt[slot])

    def wait(self, eng, tok):
        slot, val = tok
        eng.wait_ge(self.sems[slot], val)

    def wait_all(self, eng):
        for i, c in enumerate(self.cnt):
            if c > 0:
                eng.wait_ge(self.sems[i], c)


class Ctx:
    def __init__(self, nc, es):
        self.nc = nc
        self.cnt = {}
        self.sem = {}
        for name in ["TS", "VS", "SS"]:
            self.sem[name] = es.enter_context(nc.semaphore(name))
            self.cnt[name] = 0

    def inc(self, instr, name, n=1):
        instr.then_inc(self.sem[name], n)
        self.cnt[name] += n
        return self.cnt[name]

    def wait(self, eng, name, val):
        if val > 0:
            eng.wait_ge(self.sem[name], val)


def barrier(ctx, block):
    snap = dict(ctx.cnt)
    owners = {"TS": "tensor", "VS": "vector", "SS": "scalar"}

    def mk(engname):
        def f(eng):
            for s, v in snap.items():
                if v > 0 and owners[s] != engname:
                    eng.wait_ge(ctx.sem[s], v)
        return f
    block.tensor(mk("tensor"))
    block.vector(mk("vector"))
    block.scalar(mk("scalar"))
    block.gpsimd(mk("gpsimd"))
    block.sync(mk("sync"))


def build_program2():
    nc = bass.Bass("TRN2", target_bir_lowering=False)

    d_u = nc.dram_tensor("u_ext", [EXT, D], F32, kind="ExternalInput").ap()
    d_wqkv = nc.dram_tensor("wqkv_pack", [32, 128, 2048], F32, kind="ExternalInput").ap()
    d_vpk = nc.dram_tensor("v_pack", [8, 128, 4096], F32, kind="ExternalInput").ap()
    d_wo = nc.dram_tensor("wo_pack", [16, 128, 2048], F32, kind="ExternalInput").ap()
    d_wgu = nc.dram_tensor("wgu_pack", [128, 128, 2048], F32, kind="ExternalInput").ap()
    d_wd = nc.dram_tensor("wd_pack", [64, 128, 2048], F32, kind="ExternalInput").ap()
    d_bqkv_pc = nc.dram_tensor("bqkv_pc", [128, 48], F32, kind="ExternalInput").ap()
    d_bqkv_sw = nc.dram_tensor("bqkv_sw", [128, 32], F32, kind="ExternalInput").ap()
    d_vbias = nc.dram_tensor("vbias_row", [1, 2048], F32, kind="ExternalInput").ap()
    d_bo = nc.dram_tensor("bo_pc", [128, 16], F32, kind="ExternalInput").ap()
    d_sinkt = nc.dram_tensor("sinks_t", [NH, DH, NS], F32, kind="ExternalInput").ap()
    d_sinkn = nc.dram_tensor("sinks_n", [NH, NS, DH], F32, kind="ExternalInput").ap()
    d_ck = nc.dram_tensor("ck", [128, EXT], F32, kind="ExternalInput").ap()
    d_sk = nc.dram_tensor("sk", [128, EXT], F32, kind="ExternalInput").ap()
    d_masks = nc.dram_tensor("masks", [NT_OWN, 128, 272], F32, kind="ExternalInput").ap()
    d_ident = nc.dram_tensor("ident", [128, 128], F32, kind="ExternalInput").ap()
    d_ones = nc.dram_tensor("ones_row", [1, EXT], F32, kind="ExternalInput").ap()
    d_consts = nc.dram_tensor("consts", [128, 4], F32, kind="ExternalInput").ap()
    d_out = nc.dram_tensor("out", [TOK, D], F32, kind="ExternalOutput").ap()
    d_oscr = nc.dram_tensor("o_scr", [16, 128, 1024], BF16, kind="Internal").ap()

    dbg = {}
    if _DEBUG:
        dbg["xnf"] = nc.dram_tensor("d_xnf", [128, 16 * EXT], BF16, kind="ExternalOutput").ap()
        dbg["qf"] = nc.dram_tensor("d_qf", [128, 16 * TOK], BF16, kind="ExternalOutput").ap()
        dbg["kf"] = nc.dram_tensor("d_kf", [128, 16 * EXT], BF16, kind="ExternalOutput").ap()
        dbg["vt"] = nc.dram_tensor("d_vt", [128, NT_EXT * 2048], BF16, kind="ExternalOutput").ap()
        dbg["of"] = nc.dram_tensor("d_of", [16, 128, 1024], BF16, kind="ExternalOutput").ap()
        dbg["g"] = nc.dram_tensor("d_g", [128, 16 * TOK], F32, kind="ExternalOutput").ap()

    with ExitStack() as glob:
        eg = glob.enter_context
        ctx = Ctx(nc, glob)
        DBG = Slots(nc, glob, "dbg", 1) if _DEBUG else None

        xn_f = eg(nc.sbuf_tensor("xn_f", [128, 16, EXT], BF16))
        consts = eg(nc.sbuf_tensor("consts_sb", [128, 4], F32))
        bo_pc = eg(nc.sbuf_tensor("bo_sb", [128, 16], F32))
        id_f = eg(nc.sbuf_tensor("id_f", [128, 128], F32))
        id_bf = eg(nc.sbuf_tensor("id_bf", [128, 128], BF16))
        sumv = eg(nc.sbuf_tensor("sumv", [128, 1], F32))
        sumsq = eg(nc.sbuf_tensor("sumsq", [128, 1], F32))
        meanv = eg(nc.sbuf_tensor("meanv", [128, 1], F32))
        msq = eg(nc.sbuf_tensor("msq", [128, 1], F32))
        ex2 = eg(nc.sbuf_tensor("ex2", [128, 1], F32))
        varep = eg(nc.sbuf_tensor("varep", [128, 1], F32))
        sqv = eg(nc.sbuf_tensor("sqv", [128, 1], F32))
        rstd = eg(nc.sbuf_tensor("rstd", [128, 1], F32))
        tmp1 = eg(nc.sbuf_tensor("tmp1", [128, 1], F32))
        negmr = eg(nc.sbuf_tensor("negmr", [128, 1], F32))

        block = eg(nc.Block())

        CS = Slots(nc, glob, "cs", 1)
        CG = Slots(nc, glob, "cg", 1)

        def p0_sync(s):
            CS.dma(s, id_f[:], d_ident)
            CS.dma(s, consts[:], d_consts)
            CS.dma(s, bo_pc[:], d_bo)

        def p0_gp(g):
            CG.dma(g, id_bf[:], d_ident)
        block.sync(p0_sync)
        block.gpsimd(p0_gp)
        for emit in (block.tensor, block.vector, block.scalar):
            def f(eng):
                CS.wait_all(eng)
                CG.wait_all(eng)
            emit(f)

        sP1mem = glob.enter_context(ExitStack())
        u_buf = [sP1mem.enter_context(nc.sbuf_tensor(f"u_buf{i}", [128, D], F32))
                 for i in range(2)]
        xn_bf = [sP1mem.enter_context(nc.sbuf_tensor(f"xn_bf{i}", [128, D], BF16))
                 for i in range(2)]
        sq_scr = sP1mem.enter_context(nc.sbuf_tensor("sq_scr", [128, D], BF16))
        U = Slots(nc, glob, "u", 2)

        base = dict(ctx.cnt)
        utok = [None] * NT_EXT
        ss_sq = [0] * NT_EXT
        ss_sqrt = [0] * NT_EXT
        vs_stat = [0] * NT_EXT
        vs_fin = [0] * NT_EXT
        ss_xn = [0] * NT_EXT
        ts_tr = [[0] * 16 for _ in range(NT_EXT)]
        vs_ev = [[0] * 16 for _ in range(NT_EXT)]
        cSS, cVS, cTS = base["SS"], base["VS"], base["TS"]
        ucnt = [0, 0]
        for tt in range(NT_EXT):
            ucnt[tt % 2] += 16
            utok[tt] = (tt % 2, ucnt[tt % 2])
            cSS += 1; ss_sq[tt] = cSS
            cVS += 1; vs_stat[tt] = cVS
            cSS += 1; ss_sqrt[tt] = cSS
            cVS += 1; vs_fin[tt] = cVS
            cSS += 1; ss_xn[tt] = cSS
            for kc in range(16):
                cTS += 1; ts_tr[tt][kc] = cTS
                cVS += 1; vs_ev[tt][kc] = cVS

        def p1_sync(s):
            for tt in range(NT_EXT):
                if tt >= 2:
                    ctx.wait(s, "SS", ss_xn[tt - 2])
                tok = U.dma(s, u_buf[tt % 2][:], d_u[tt * 128:(tt + 1) * 128, :],
                            tt % 2)
                assert tok == utok[tt]
        block.sync(p1_sync)

        def p1_scalar(sc):
            for tt in range(NT_EXT):
                U.wait(sc, utok[tt])
                ctx.inc(sc.activation(sq_scr[:], u_buf[tt % 2][:], AF.Square,
                                      accum_out=sumsq[:]), "SS")
                ctx.wait(sc, "VS", vs_stat[tt])
                ctx.inc(sc.activation(sqv[:], varep[:], AF.Sqrt), "SS")
                ctx.wait(sc, "VS", vs_fin[tt])
                if tt >= 2:
                    ctx.wait(sc, "TS", ts_tr[tt - 2][15])
                ctx.inc(sc.activation(xn_bf[tt % 2][:], u_buf[tt % 2][:], AF.Identity,
                                      bias=negmr[:], scale=rstd[:]), "SS")
        block.scalar(p1_scalar)

        def p1_tensor(t):
            for tt in range(NT_EXT):
                ctx.wait(t, "SS", ss_xn[tt])
                for kc in range(16):
                    if tt * 16 + kc >= 2:
                        prev = tt * 16 + kc - 2
                        ctx.wait(t, "VS", vs_ev[prev // 16][prev % 16])
                    ctx.inc(t.transpose(ps_tr[kc % 2][:],
                                        xn_bf[tt % 2][:, kc * 128:(kc + 1) * 128],
                                        id_bf[:]), "TS")

        def p1_vector(v):
            for tt in range(NT_EXT):
                U.wait(v, utok[tt])
                ub = u_buf[tt % 2]
                v.reduce_sum(sumv[:], ub[:], axis=AX)
                v.drain()
                v.tensor_mul(meanv[:], sumv[:], consts[:, 2:3])
                v.drain()
                v.tensor_mul(msq[:], meanv[:], meanv[:])
                ctx.wait(v, "SS", ss_sq[tt])
                v.tensor_mul(ex2[:], sumsq[:], consts[:, 2:3])
                v.drain()
                v.tensor_sub(tmp1[:], ex2[:], msq[:])
                v.drain()
                ctx.inc(v.tensor_add(varep[:], tmp1[:], consts[:, 0:1]), "VS")
                ctx.wait(v, "SS", ss_sqrt[tt])
                v.reciprocal(rstd[:], sqv[:])
                v.drain()
                v.tensor_mul(tmp1[:], meanv[:], rstd[:])
                v.drain()
                ctx.inc(v.tensor_mul(negmr[:], tmp1[:], consts[:, 1:2]), "VS")
                for kc in range(16):
                    ctx.wait(v, "TS", ts_tr[tt][kc])
                    ctx.inc(v.tensor_copy(xn_f[:, kc, tt * 128:(tt + 1) * 128],
                                          ps_tr[kc % 2][:]), "VS")
        with ExitStack() as sP1:
            ps_tr = [sP1.enter_context(nc.psum_tensor(f"ps_tr{i}", [128, 128], BF16))
                     for i in range(2)]
            block.tensor(p1_tensor)
            block.vector(p1_vector)
        assert ctx.cnt["SS"] == cSS and ctx.cnt["VS"] == cVS and ctx.cnt["TS"] == cTS
        barrier(ctx, block)

        if _DEBUG:
            def dmp(s):
                DBG.dma(s, dbg["xnf"], xn_f[:].rearrange("p a b -> p (a b)"))
                DBG.wait_all(s)
            block.sync(dmp)
            for emitf in (block.tensor, block.vector, block.scalar, block.gpsimd):
                def fdbg1(eng):
                    DBG.wait_all(eng)
                emitf(fdbg1)
            barrier(ctx, block)
        if _PHASES <= 1:
            return nc

        sP1mem.close()
        OS_outer = [None]
        with ExitStack() as sA:
            ea = sA.enter_context
            q_f = ea(nc.sbuf_tensor("q_f", [128, 16, TOK], BF16))
            k_f = ea(nc.sbuf_tensor("k_f", [128, 16, EXT], BF16))
            v_t = ea(nc.sbuf_tensor("v_t", [128, NT_EXT, 2048], BF16))
            ck_t = ea(nc.sbuf_tensor("ck_t", [128, EXT], F32))
            sk_t = ea(nc.sbuf_tensor("sk_t", [128, EXT], F32))
            wx = [ea(nc.sbuf_tensor(f"wx{i}", [128, 4096], BF16)) for i in range(3)]
            t1 = ea(nc.sbuf_tensor("t1", [128, 512], F32))
            m2 = ea(nc.sbuf_tensor("m2", [128, 512], F32))
            t2sw = [ea(nc.sbuf_tensor(f"t2sw{i}", [128, 512], F32)) for i in range(2)]
            ones_bf = ea(nc.sbuf_tensor("ones_bf", [1, EXT], BF16))
            bq_pc = ea(nc.sbuf_tensor("bq_pc", [128, 48], F32))
            bq_sw = ea(nc.sbuf_tensor("bq_sw", [128, 32], F32))
            vbias_bf = ea(nc.sbuf_tensor("vbias_bf", [1, 2048], BF16))
            sinkt_bf = ea(nc.sbuf_tensor("sinkt_bf", [128, NH * NS], BF16))
            sinkn_bf = ea(nc.sbuf_tensor("sinkn_bf", [NS, NH * DH], BF16))
            masks_bf = ea(nc.sbuf_tensor("masks_bf", [128, NT_OWN, 272], BF16))
            o_buf = [ea(nc.sbuf_tensor(f"o_buf{i}", [128, 1024], BF16)) for i in range(2)]
            pA_bf = [ea(nc.sbuf_tensor(f"pA_bf{i}", [128, 272], BF16)) for i in range(2)]
            pB_bf = [ea(nc.sbuf_tensor(f"pB_bf{i}", [128, 272], BF16)) for i in range(2)]
            pt_sb = [ea(nc.sbuf_tensor(f"pt_sb{i}", [128, 512], BF16)) for i in range(2)]
            pts_sb = [ea(nc.sbuf_tensor(f"pts_sb{i}", [NS, 256], BF16)) for i in range(2)]
            nmA = [ea(nc.sbuf_tensor(f"nmA{i}", [128, 1], F32)) for i in range(2)]
            nmB = [ea(nc.sbuf_tensor(f"nmB{i}", [128, 1], F32)) for i in range(2)]
            n2A = [ea(nc.sbuf_tensor(f"n2A{i}", [128, 1], F32)) for i in range(2)]
            n2B = [ea(nc.sbuf_tensor(f"n2B{i}", [128, 1], F32)) for i in range(2)]
            rsA = [ea(nc.sbuf_tensor(f"rsA{i}", [128, 1], F32)) for i in range(2)]
            rsB = [ea(nc.sbuf_tensor(f"rsB{i}", [128, 1], F32)) for i in range(2)]
            rcA = [ea(nc.sbuf_tensor(f"rcA{i}", [128, 1], F32)) for i in range(2)]
            rcB = [ea(nc.sbuf_tensor(f"rcB{i}", [128, 1], F32)) for i in range(2)]

            TB = Slots(nc, glob, "tb", 1)
            CB = Slots(nc, glob, "cb", 1)
            MS = Slots(nc, glob, "ms", 1)
            WX = Slots(nc, glob, "wxs", 3)
            OS = Slots(nc, glob, "os", 2)
            OS_outer[0] = OS

            def p2_consts(g):
                CB.dma(g, ones_bf[:], d_ones)
                CB.dma(g, vbias_bf[:], d_vbias)
                MS.dma(g, sinkt_bf[0:64, :].rearrange("d (h s) -> d h s", h=NH),
                       d_sinkt.rearrange("h d s -> d h s"))
                MS.dma(g, sinkt_bf[64:128, :].rearrange("d (h s) -> d h s", h=NH),
                       d_sinkt.rearrange("h d s -> d h s"))
                MS.dma(g, sinkn_bf[:].rearrange("s (h d) -> s h d", h=NH),
                       d_sinkn.rearrange("h s d -> s h d"))
                MS.dma(g, masks_bf[:], d_masks.rearrange("q p c -> p q c"))
            block.gpsimd(p2_consts)

            def p2_tables(s):
                TB.dma(s, ck_t[:], d_ck)
                TB.dma(s, sk_t[:], d_sk)
                TB.dma(s, bq_pc[:], d_bqkv_pc)
                TB.dma(s, bq_sw[:], d_bqkv_sw)
            block.sync(p2_tables)

            q_ts = [(0, 512), (512, 512)]
            k_ts = [(0, 512), (512, 512), (1024, 128)]

            loads = []
            units = []
            unit_of_step = [[] for _ in range(16)]
            for c in range(16):
                loads.append(("q", c))
                li_q = len(loads) - 1
                for tsi, (off, ln) in enumerate(q_ts):
                    unit_of_step[c].append(("qk", c, tsi, off, ln, li_q))
                loads.append(("k", 16 + c))
                li_k = len(loads) - 1
                for tsi, (off, ln) in enumerate(k_ts):
                    unit_of_step[c].append(("qk", 16 + c, tsi, off, ln, li_k))
                if c % 2 == 0:
                    loads.append(("v", c // 2))
                    li_v = len(loads) - 1
                    for tt in range(NT_EXT):
                        unit_of_step[c].append(("v", c // 2, tt, li_v))
            for c in range(16):
                units.extend(unit_of_step[c])
            NU = len(units)
            unit_idx = {u: i for i, u in enumerate(units)}
            last_unit_of_load = {}
            for i, u in enumerate(units):
                last_unit_of_load[u[-1]] = i

            pairs = [(c, qt) for c in range(16) for qt in range(NT_OWN)]
            NP3 = len(pairs)
            rope_done_of_chunk = [0] * 16

            base = dict(ctx.cnt)
            cTS, cVS, cSS = base["TS"], base["VS"], base["SS"]
            wxtok = [None] * len(loads)
            ts_mm = [0] * NU
            vs_t1 = [0] * NU
            ss_sw = [0] * NU
            vs_rope = [0] * NU
            ts_sc = [0] * NP3
            ts_tp = [0] * NP3
            ts_av = [0] * NP3
            vs_mx = [0] * NP3
            vs_nm = [0] * NP3
            vs_pt = [0] * NP3
            ss_ex = [0] * NP3
            ss_oe = [0] * NP3
            ostok = [None] * 16
            oc_ = [0, 0]
            wxc = [0, 0, 0]
            for li in range(len(loads)):
                sl = li % 3
                wxc[sl] += 16
                wxtok[li] = (sl, wxc[sl])
            for c in range(16):
                oc_[c % 2] += 16
                ostok[c] = (c % 2, oc_[c % 2])

            def win_seq(c):
                if c == 0:
                    return []
                w0, w1 = (c - 1) * NT_OWN, c * NT_OWN
                seq = []
                for j in range(w0, w1):
                    seq.append(("sc", j))
                    if j - 1 >= w0:
                        seq.append(("tp", j - 1))
                    if j - 2 >= w0:
                        seq.append(("av", j - 2))
                seq += [("av", w1 - 2), ("tp", w1 - 1), ("av", w1 - 1)]
                return seq

            def p3_window(c):
                if c == 0:
                    return []
                return list(range((c - 1) * NT_OWN, c * NT_OWN))

            for c in range(17):
                if c < 16:
                    for u in unit_of_step[c]:
                        cTS += 1; ts_mm[unit_idx[u]] = cTS
                for kind, j in win_seq(c):
                    if kind == "sc":
                        cTS += 1; ts_sc[j] = cTS
                    elif kind == "tp":
                        cTS += 1; ts_tp[j] = cTS
                    else:
                        cTS += 1; ts_av[j] = cTS
            for c in range(17):
                if c < 16:
                    for u in unit_of_step[c]:
                        i = unit_idx[u]
                        if u[0] == "qk":
                            cVS += 1; vs_t1[i] = cVS
                            cVS += 1; vs_rope[i] = cVS
                            rope_done_of_chunk[c] = cVS
                        else:
                            vs_t1[i] = cVS
                            vs_rope[i] = cVS
                win = p3_window(c)
                for j in win:
                    cVS += 1; vs_mx[j] = cVS
                    cVS += 1; vs_nm[j] = cVS
                    cVS += 1; vs_pt[j] = cVS
            sc_sched = []
            for c in range(17):
                if c < 16:
                    for u in unit_of_step[c]:
                        sc_sched.append(("u", unit_idx[u]))
                win = p3_window(c)
                for j in win:
                    sc_sched.append(("p", j))
                if win:
                    w1 = win[-1] + 1
                    sc_sched.append(("o", w1 - 2))
                    sc_sched.append(("o", w1 - 1))
            for kind, x in sc_sched:
                if kind == "u":
                    cSS += 1; ss_sw[x] = cSS
                elif kind == "p":
                    cSS += 1; ss_ex[x] = cSS
                    if x - 2 >= (x // NT_OWN) * NT_OWN:
                        cSS += 1; ss_oe[x - 2] = cSS
                else:
                    cSS += 1; ss_oe[x] = cSS

            def pm_gp(g):
                for li, ld in enumerate(loads):
                    if li >= 3:
                        ctx.wait(g, "TS", ts_mm[last_unit_of_load[li - 3]])
                    kind = ld[0]
                    if kind == "q":
                        src = d_wqkv[ld[1]]
                        tok = WX.dma(g, wx[li % 3][:, 0:2048], src, li % 3)
                    elif kind == "k":
                        src = d_wqkv[ld[1]]
                        tok = WX.dma(g, wx[li % 3][:, 0:2048], src, li % 3)
                    else:
                        src = d_vpk[ld[1]]
                        tok = WX.dma(g, wx[li % 3][:], src, li % 3)
                    assert tok == wxtok[li]
            block.gpsimd(pm_gp)

            def emit_unit_mm(t, u):
                i = unit_idx[u]
                if u[0] == "qk":
                    _, oc, tsi, off, ln, li = u
                    if tsi == 0:
                        WX.wait(t, wxtok[li])
                    if i >= 2:
                        ctx.wait(t, "VS", vs_rope[i - 2])
                        ctx.wait(t, "SS", ss_sw[i - 2])
                    p = ps_qk[i % 2]
                    if oc < 16:
                        src = xn_f[:, :, HALO + off:HALO + off + ln]
                    else:
                        src = xn_f[:, :, off:off + ln]
                    w = wx[li % 3]
                    for kc in range(16):
                        ins = t.matmul(p[:, 0:ln], w[:, kc * 128:(kc + 1) * 128],
                                       src[:, kc, :], start=(kc == 0), stop=(kc == 15))
                    ctx.inc(ins, "TS")
                else:
                    _, gp, tt, li = u
                    if tt == 0:
                        WX.wait(t, wxtok[li])
                    if i >= 2:
                        ctx.wait(t, "VS", vs_rope[i - 2])
                        ctx.wait(t, "SS", ss_sw[i - 2])
                    p = ps_qk[i % 2]
                    w = wx[li % 3]
                    for kc in range(16):
                        t.matmul(p[:, 0:256], xn_f[:, kc, tt * 128:(tt + 1) * 128],
                                 w[:, kc * 256:(kc + 1) * 256],
                                 start=(kc == 0), stop=False)
                    ctx.inc(t.matmul(p[:, 0:256], ones_bf[0:1, 0:128],
                                     vbias_bf[0:1, gp * 256:(gp + 1) * 256],
                                     start=False, stop=True), "TS")

            def p3_emit_scores(t, j):
                c, qt = pairs[j]
                if j >= 2:
                    ctx.wait(t, "SS", ss_ex[j - 2])
                if qt == 0:
                    ctx.wait(t, "VS", rope_done_of_chunk[c])
                for hh in range(2):
                    h = 2 * c + hh
                    po = hh * 64
                    qsl = q_f[po:po + 64, c, qt * 128:(qt + 1) * 128]
                    ksl = k_f[po:po + 64, c, qt * 128:qt * 128 + 256]
                    ps = (psA if hh == 0 else psB)[j % 2]
                    t.matmul(ps[:], id_bf[:], masks_bf[:, qt, :], start=True, stop=False)
                    t.matmul(ps[:, 0:NS], qsl, sinkt_bf[po:po + 64, h * NS:(h + 1) * NS],
                             start=False, stop=False)
                    ins = t.matmul(ps[:, NS:272], qsl, ksl, start=False, stop=True)
                ctx.inc(ins, "TS")

            def p3_emit_tp(t, i):
                ctx.wait(t, "VS", vs_nm[i])
                if i >= 1:
                    ctx.wait(t, "VS", vs_pt[i - 1])
                pt = ps_tp
                pa, pb = pA_bf[i % 2], pB_bf[i % 2]
                t.transpose(pt[:, 0:128], pa[:, NS:NS + 128], id_bf[:])
                t.transpose(pt[:, 128:256], pa[:, NS + 128:NS + 256], id_bf[:])
                t.transpose(pt[0:NS, 256:384], pa[:, 0:NS], id_bf[:])
                t.transpose(pt[:, 384:512], pb[:, NS:NS + 128], id_bf[:])
                t.transpose(pt[:, 512:640], pb[:, NS + 128:NS + 256], id_bf[:])
                ins = t.transpose(pt[0:NS, 640:768], pb[:, 0:NS], id_bf[:])
                ctx.inc(ins, "TS")

            def p3_emit_av(t, i):
                c, qt = pairs[i]
                ctx.wait(t, "VS", vs_pt[i])
                if i >= 1:
                    ctx.wait(t, "SS", ss_oe[i - 1])
                vu = [u for u in unit_of_step[2 * (c // 2)] if u[0] == "v"][-1]
                ctx.wait(t, "SS", ss_sw[unit_idx[vu]])
                for hh in range(2):
                    h = 2 * c + hh
                    co = hh * 256
                    tp_ = (0, 0) if hh == 0 else (0, 64)
                    outp = ps_o[0:64, :] if hh == 0 else ps_o[64:128, :]
                    t.matmul(outp, v_t[:, qt, h * 64:(h + 1) * 64],
                             pt_sb[i % 2][:, co:co + 128], start=True, stop=False,
                             tile_position=tp_)
                    t.matmul(outp, v_t[:, qt + 1, h * 64:(h + 1) * 64],
                             pt_sb[i % 2][:, co + 128:co + 256], start=False, stop=False,
                             tile_position=tp_)
                    ins = t.matmul(outp, sinkn_bf[:, h * 64:(h + 1) * 64],
                                   pts_sb[i % 2][:, hh * 128:hh * 128 + 128],
                                   start=False, stop=True, tile_position=tp_)
                ctx.inc(ins, "TS")

            def pm_tensor(t):
                CB.wait_all(t)
                MS.wait_all(t)
                for c in range(17):
                    if c < 16:
                        for u in unit_of_step[c]:
                            emit_unit_mm(t, u)
                    for kind, j in win_seq(c):
                        if kind == "sc":
                            p3_emit_scores(t, j)
                        elif kind == "tp":
                            p3_emit_tp(t, j)
                        else:
                            p3_emit_av(t, j)
            def pm_vector(v):
                TB.wait_all(v)
                for c in range(17):
                    if c < 16:
                        for u in unit_of_step[c]:
                            i = unit_idx[u]
                            if u[0] != "qk":
                                continue
                            _, oc, tsi, off, ln, li = u
                            p = ps_qk[i % 2]
                            sw = t2sw[i % 2]
                            ctx.wait(v, "TS", ts_mm[i])
                            if oc < 16:
                                cs, ss_ = (ck_t[:, HALO + off:HALO + off + ln],
                                           sk_t[:, HALO + off:HALO + off + ln])
                                dst = q_f[:, oc, off:off + ln]
                            else:
                                cs, ss_ = (ck_t[:, off:off + ln],
                                           sk_t[:, off:off + ln])
                                dst = k_f[:, oc - 16, off:off + ln]
                            ctx.inc(v.scalar_tensor_tensor(
                                t1[:, 0:ln], p[:, 0:ln], bq_pc[:, oc:oc + 1], cs,
                                OP.add, OP.mult), "VS")
                            ctx.wait(v, "SS", ss_sw[i])
                            v.scalar_tensor_tensor(
                                m2[:, 0:ln], sw[:, 0:ln], bq_sw[:, oc:oc + 1], ss_,
                                OP.add, OP.mult)
                            v.drain()
                            ctx.inc(v.tensor_add(dst, t1[:, 0:ln], m2[:, 0:ln]), "VS")
                            v.drain()
                    for j in p3_window(c):
                        ctx.wait(v, "TS", ts_sc[j])
                        v.reduce_max(nmA[j % 2][:], psA[j % 2][:], axis=AX, negate=True)
                        v.reduce_max(nmB[j % 2][:], psB[j % 2][:], axis=AX, negate=True)
                        v.drain()
                        v.tensor_mul(n2A[j % 2][:], nmA[j % 2][:], consts[:, 3:4])
                        ctx.inc(v.tensor_mul(n2B[j % 2][:], nmB[j % 2][:],
                                             consts[:, 3:4]), "VS")
                        ctx.wait(v, "SS", ss_ex[j])
                        v.reciprocal(rcA[j % 2][:], rsA[j % 2][:])
                        v.reciprocal(rcB[j % 2][:], rsB[j % 2][:])
                        v.drain()
                        v.tensor_scalar_mul(pA_bf[j % 2][:], pA_bf[j % 2][:],
                                            rcA[j % 2][:])
                        ctx.inc(v.tensor_scalar_mul(pB_bf[j % 2][:], pB_bf[j % 2][:],
                                                    rcB[j % 2][:]), "VS")
                        ctx.wait(v, "TS", ts_tp[j])
                        v.tensor_copy(pt_sb[j % 2][:, 0:256], ps_tp[:, 0:256])
                        v.tensor_copy(pt_sb[j % 2][:, 256:512], ps_tp[:, 384:640])
                        v.tensor_copy(pts_sb[j % 2][:, 0:128], ps_tp[0:NS, 256:384])
                        ctx.inc(v.tensor_copy(pts_sb[j % 2][:, 128:256],
                                              ps_tp[0:NS, 640:768]), "VS")
            def pm_scalar(sc):
                for kind, x in sc_sched:
                    if kind == "u":
                        u = units[x]
                        i = x
                        p = ps_qk[i % 2]
                        if u[0] == "qk":
                            sw = t2sw[i % 2]
                            ctx.wait(sc, "VS", vs_t1[i])
                            if i >= 2:
                                ctx.wait(sc, "VS", vs_rope[i - 2])
                            _, oc, tsi, off, ln, li = u
                            sc.copy(sw[0:32, 0:ln], p[32:64, 0:ln])
                            sc.copy(sw[32:64, 0:ln], p[0:32, 0:ln])
                            sc.copy(sw[64:96, 0:ln], p[96:128, 0:ln])
                            ctx.inc(sc.copy(sw[96:128, 0:ln], p[64:96, 0:ln]), "SS")
                        else:
                            _, gp, tt, li = u
                            ctx.wait(sc, "TS", ts_mm[i])
                            ctx.inc(sc.activation(v_t[:, tt, gp * 256:(gp + 1) * 256],
                                                  p[:, 0:256], AF.Copy), "SS")
                    else:
                        def emit_oevac(i):
                            c, qt = pairs[i]
                            ctx.wait(sc, "TS", ts_av[i])
                            if qt == 0 and c >= 2:
                                OS.wait(sc, ostok[c - 2])
                            sc.activation(o_buf[c % 2][0:64, qt * 128:(qt + 1) * 128],
                                          ps_o[0:64, :], AF.Copy)
                            ctx.inc(sc.activation(o_buf[c % 2][64:128,
                                                             qt * 128:(qt + 1) * 128],
                                                  ps_o[64:128, :], AF.Copy), "SS")
                        if kind == "p":
                            j = x
                            ctx.wait(sc, "VS", vs_mx[j])
                            if j >= 2:
                                ctx.wait(sc, "TS", ts_tp[j - 2])
                            sc.activation(pA_bf[j % 2][:], psA[j % 2][:], AF.Exp,
                                          bias=n2A[j % 2][:], scale=consts[:, 3:4],
                                          accum_out=rsA[j % 2][:])
                            ctx.inc(sc.activation(pB_bf[j % 2][:], psB[j % 2][:],
                                                  AF.Exp, bias=n2B[j % 2][:],
                                                  scale=consts[:, 3:4],
                                                  accum_out=rsB[j % 2][:]), "SS")
                            if x - 2 >= (x // NT_OWN) * NT_OWN:
                                emit_oevac(x - 2)
                        else:
                            emit_oevac(x)

            def pm_sync(s):
                for c in range(16):
                    last_i = c * NT_OWN + (NT_OWN - 1)
                    ctx.wait(s, "SS", ss_oe[last_i])
                    tok = OS.dma(s, d_oscr[c], o_buf[c % 2][:], c % 2)
                    assert tok == ostok[c]

            with ExitStack() as sPM:
                ps_qk = [sPM.enter_context(nc.psum_tensor(f"ps_qk{i}", [128, 512], F32))
                         for i in range(2)]
                psA = [sPM.enter_context(nc.psum_tensor(f"psA{i}", [128, 272], F32))
                       for i in range(2)]
                psB = [sPM.enter_context(nc.psum_tensor(f"psB{i}", [128, 272], F32))
                       for i in range(2)]
                ps_tp = sPM.enter_context(nc.psum_tensor("ps_tp", [128, 768], BF16))
                ps_o = sPM.enter_context(nc.psum_tensor("ps_o", [128, 128], F32))
                block.tensor(pm_tensor)
                block.vector(pm_vector)
                block.scalar(pm_scalar)
                block.sync(pm_sync)
                assert ctx.cnt["TS"] == cTS and ctx.cnt["VS"] == cVS
                assert ctx.cnt["SS"] == cSS
            barrier(ctx, block)
            if _DEBUG:
                def dmp2(s):
                    DBG.dma(s, dbg["qf"], q_f[:].rearrange("p a b -> p (a b)"))
                    DBG.dma(s, dbg["kf"], k_f[:].rearrange("p a b -> p (a b)"))
                    DBG.dma(s, dbg["vt"], v_t[:].rearrange("p a b -> p (a b)"))
                    OS.wait_all(s)
                    DBG.dma(s, dbg["of"], d_oscr)
                    DBG.wait_all(s)
                block.sync(dmp2)
                for emitf in (block.tensor, block.vector, block.scalar, block.gpsimd):
                    def fdbg(eng):
                        DBG.wait_all(eng)
                    emitf(fdbg)
                barrier(ctx, block)
        if _PHASES <= 3:
            return nc

        sB = glob.enter_context(ExitStack())
        g_sb = sB.enter_context(nc.sbuf_tensor("g_sb", [128, 16, TOK], F32))
        with ExitStack() as sC:
            ec = sC.enter_context
            h_ib = ec(nc.sbuf_tensor("h_ib", [128, 8, TOK], BF16))
            wd_ib = ec(nc.sbuf_tensor("wd_ib", [128, 8, 2048], BF16))
            gbuf = [ec(nc.sbuf_tensor(f"gbuf{i}", [128, 2048], BF16)) for i in range(2)]
            ubuf = [ec(nc.sbuf_tensor(f"ubuf{i}", [128, 2048], BF16)) for i in range(2)]
            tg = [ec(nc.sbuf_tensor(f"tg{i}", [128, 512], F32)) for i in range(2)]
            G = Slots(nc, glob, "gg", 2)
            UU = Slots(nc, glob, "uu", 2)
            WD = Slots(nc, glob, "wd", 1)

            base = dict(ctx.cnt)
            gtok = [[None] * 8 for _ in range(8)]
            utok2 = [[None] * 8 for _ in range(8)]
            wdtok = [None] * 8
            ts_gt = {}
            ts_up = {}
            ss_si = {}
            vs_h = {}
            ts_dn = {}
            vs_ge = {}
            cTS, cSS, cVS = base["TS"], base["SS"], base["VS"]
            gc_ = [0, 0]
            uc_ = [0, 0]
            wdc = 0
            for ib in range(8):
                for ic in range(8):
                    gc_[ic % 2] += 16; gtok[ib][ic] = (ic % 2, gc_[ic % 2])
                    uc_[ic % 2] += 16; utok2[ib][ic] = (ic % 2, uc_[ic % 2])
                wdc += 16 * 8; wdtok[ib] = (0, wdc)
                for ic in range(8):
                    for tsx in range(2):
                        cTS += 1; ts_gt[(ib, ic, tsx)] = cTS
                        cSS += 1; ss_si[(ib, ic, tsx)] = cSS
                        cTS += 1; ts_up[(ib, ic, tsx)] = cTS
                        cVS += 1; vs_h[(ib, ic, tsx)] = cVS
                for tsx in range(2):
                    for oc in range(16):
                        cTS += 1; ts_dn[(ib, tsx, oc)] = cTS
                        cVS += 1; vs_ge[(ib, tsx, oc)] = cVS

            ts_keys_g = [(ib, ic, tsx) for ib in range(8) for ic in range(8)
                         for tsx in range(2)]
            dn_keys = [(ib, tsx, oc) for ib in range(8) for tsx in range(2)
                       for oc in range(16)]

            def p5_gp(g):
                OS_outer[0].wait_all(g)
                for ib in range(8):
                    for ic in range(8):
                        pic = ib * 8 + ic - 2
                        if pic >= 0:
                            ctx.wait(g, "TS", ts_gt[(pic // 8, pic % 8, 1)])
                        tok = G.dma(g, gbuf[ic % 2][:], d_wgu[ib * 16 + ic * 2], ic % 2)
                        assert tok == gtok[ib][ic]
                        if pic >= 0:
                            ctx.wait(g, "TS", ts_up[(pic // 8, pic % 8, 1)])
                        tok = UU.dma(g, ubuf[ic % 2][:], d_wgu[ib * 16 + ic * 2 + 1],
                                     ic % 2)
                        assert tok == utok2[ib][ic]
                    if ib >= 1:
                        ctx.wait(g, "TS", ts_dn[(ib - 1, 1, 15)])
                    for ic in range(8):
                        tok = WD.dma(g, wd_ib[:, ic, :], d_wd[ib * 8 + ic], 0)
                    assert tok == wdtok[ib]
            block.gpsimd(p5_gp)

            def p5_tensor(t):
                for ib in range(8):
                    for ic in range(8):
                        for tsx in range(2):
                            gi = (ib * 8 + ic) * 2 + tsx
                            if tsx == 0:
                                G.wait(t, gtok[ib][ic])
                            if gi >= 2:
                                pk = ts_keys_g[gi - 2]
                                ctx.wait(t, "SS", ss_si[pk])
                            pg = ps_g[gi % 2]
                            for kc in range(16):
                                ins = t.matmul(
                                    pg[:], gbuf[ic % 2][:, kc * 128:(kc + 1) * 128],
                                    xn_f[:, kc, HALO + tsx * 512:HALO + (tsx + 1) * 512],
                                    start=(kc == 0), stop=(kc == 15))
                            ctx.inc(ins, "TS")
                            if tsx == 0:
                                UU.wait(t, utok2[ib][ic])
                            if gi >= 2:
                                pk = ts_keys_g[gi - 2]
                                ctx.wait(t, "VS", vs_h[pk])
                            pu = ps_u[gi % 2]
                            for kc in range(16):
                                ins = t.matmul(
                                    pu[:], ubuf[ic % 2][:, kc * 128:(kc + 1) * 128],
                                    xn_f[:, kc, HALO + tsx * 512:HALO + (tsx + 1) * 512],
                                    start=(kc == 0), stop=(kc == 15))
                            ctx.inc(ins, "TS")
                    WD.wait(t, wdtok[ib])
                    for tsx in range(2):
                        for oc in range(16):
                            j = (ib * 2 + tsx) * 16 + oc
                            if j >= 2:
                                ctx.wait(t, "VS", vs_ge[dn_keys[j - 2]])
                            ctx.wait(t, "VS", vs_h[(ib, 7, tsx)])
                            pd = ps_d[j % 2]
                            for ic in range(8):
                                ins = t.matmul(
                                    pd[:], wd_ib[:, ic, oc * 128:(oc + 1) * 128],
                                    h_ib[:, ic, tsx * 512:(tsx + 1) * 512],
                                    start=(ic == 0), stop=(ic == 7))
                            ctx.inc(ins, "TS")

            def p5_scalar(sc):
                for ib in range(8):
                    for ic in range(8):
                        for tsx in range(2):
                            gi = (ib * 8 + ic) * 2 + tsx
                            ctx.wait(sc, "TS", ts_gt[(ib, ic, tsx)])
                            if gi >= 2:
                                pk = ts_keys_g[gi - 2]
                                ctx.wait(sc, "VS", vs_h[pk])
                            ctx.inc(sc.activation(tg[gi % 2][:], ps_g[gi % 2][:],
                                                  AF.Silu), "SS")

            def p5_vector(v):
                for ib in range(8):
                    for ic in range(8):
                        for tsx in range(2):
                            gi = (ib * 8 + ic) * 2 + tsx
                            ctx.wait(v, "TS", ts_up[(ib, ic, tsx)])
                            ctx.wait(v, "SS", ss_si[(ib, ic, tsx)])
                            ctx.inc(v.tensor_mul(h_ib[:, ic, tsx * 512:(tsx + 1) * 512],
                                                 tg[gi % 2][:], ps_u[gi % 2][:]), "VS")
                    for tsx in range(2):
                        for oc in range(16):
                            j = (ib * 2 + tsx) * 16 + oc
                            ctx.wait(v, "TS", ts_dn[(ib, tsx, oc)])
                            dst = g_sb[:, oc, tsx * 512:(tsx + 1) * 512]
                            if ib == 0:
                                ins = v.tensor_copy(dst, ps_d[j % 2][:])
                            else:
                                ins = v.tensor_add(dst, dst, ps_d[j % 2][:])
                            ctx.inc(ins, "VS")

            with ExitStack() as sP5:
                ps_g = [sP5.enter_context(nc.psum_tensor(f"ps_g{i}", [128, 512], F32))
                        for i in range(2)]
                ps_u = [sP5.enter_context(nc.psum_tensor(f"ps_u{i}", [128, 512], F32))
                        for i in range(2)]
                ps_d = [sP5.enter_context(nc.psum_tensor(f"ps_d{i}", [128, 512], F32))
                        for i in range(2)]
                block.tensor(p5_tensor)
                block.scalar(p5_scalar)
                block.vector(p5_vector)
                assert ctx.cnt["TS"] == cTS and ctx.cnt["SS"] == cSS
                assert ctx.cnt["VS"] == cVS
            barrier(ctx, block)

        if _DEBUG:
            def dmp5(s):
                DBG.dma(s, dbg["g"], g_sb[:].rearrange("p a b -> p (a b)"))
                DBG.wait_all(s)
            block.sync(dmp5)
            for emitf in (block.tensor, block.vector, block.scalar, block.gpsimd):
                def fdbg5(eng):
                    DBG.wait_all(eng)
                emitf(fdbg5)
            barrier(ctx, block)
        if _PHASES <= 5:
            return nc

        with ExitStack() as sD:
            ed = sD.enter_context
            o_sb = ed(nc.sbuf_tensor("o_sb", [128, 16, 1024], BF16))
            wobuf = [ed(nc.sbuf_tensor(f"wobuf{i}", [128, 2048], BF16)) for i in range(2)]
            gsil = [ed(nc.sbuf_tensor(f"gsil{i}", [128, 512], F32)) for i in range(2)]
            u2_buf = [ed(nc.sbuf_tensor(f"u2_buf{i}", [128, D], F32)) for i in range(2)]
            out_buf = [ed(nc.sbuf_tensor(f"out_buf{i}", [128, D], F32)) for i in range(2)]
            WO = Slots(nc, glob, "wo", 2)
            OSB = Slots(nc, glob, "osb", 1)
            U2 = Slots(nc, glob, "u2", 2)
            OUT = Slots(nc, glob, "oo", 2)

            base = dict(ctx.cnt)
            cTS, cSS, cVS = base["TS"], base["SS"], base["VS"]
            wotok = [None] * 16
            ts_s = {}
            ss_gs = {}
            vs_sg = {}
            woc = [0, 0]
            for oc in range(16):
                woc[oc % 2] += 16; wotok[oc] = (oc % 2, woc[oc % 2])
                for tsx in range(2):
                    cTS += 1; ts_s[(oc, tsx)] = cTS
                    cSS += 1; ss_gs[(oc, tsx)] = cSS
                    cVS += 1; vs_sg[(oc, tsx)] = cVS
            vs_sg_all = cVS
            u2tok = [None] * NT_OWN
            outtok = [None] * NT_OWN
            ts_tp6 = [[0] * 16 for _ in range(NT_OWN)]
            vs_add = [[0] * 16 for _ in range(NT_OWN)]
            u2c = [0, 0]
            outc = [0, 0]
            for tt in range(NT_OWN):
                u2c[tt % 2] += 16; u2tok[tt] = (tt % 2, u2c[tt % 2])
                for oc in range(16):
                    cTS += 1; ts_tp6[tt][oc] = cTS
                    cVS += 1; vs_add[tt][oc] = cVS
                outc[tt % 2] += 16; outtok[tt] = (tt % 2, outc[tt % 2])

            def p6_sync(s):
                OS.wait_all(s)
                OSB.dma(s, o_sb[:], d_oscr.rearrange("a p b -> p a b"))
                for tt in range(NT_OWN):
                    if tt >= 2:
                        ctx.wait(s, "VS", vs_add[tt - 2][15])
                    tok = U2.dma(s, u2_buf[tt % 2][:],
                                 d_u[HALO + tt * 128:HALO + (tt + 1) * 128, :], tt % 2)
                    assert tok == u2tok[tt]
                    if tt >= 1:
                        ctx.wait(s, "VS", vs_add[tt - 1][15])
                        tok = OUT.dma(s, d_out[(tt - 1) * 128:tt * 128, :],
                                      out_buf[(tt - 1) % 2][:], (tt - 1) % 2)
                        assert tok == outtok[tt - 1]
                ctx.wait(s, "VS", vs_add[NT_OWN - 1][15])
                tok = OUT.dma(s, d_out[(NT_OWN - 1) * 128:NT_OWN * 128, :],
                              out_buf[(NT_OWN - 1) % 2][:], (NT_OWN - 1) % 2)
                assert tok == outtok[NT_OWN - 1]
                OUT.wait_all(s)

            def p6_gp(g):
                for oc in range(16):
                    if oc >= 2:
                        ctx.wait(g, "TS", ts_s[(oc - 2, 1)])
                    tok = WO.dma(g, wobuf[oc % 2][:], d_wo[oc], oc % 2)
                    assert tok == wotok[oc]
            block.gpsimd(p6_gp)

            def p6_tensor(t):
                OSB.wait_all(t)
                for oc in range(16):
                    WO.wait(t, wotok[oc])
                    for tsx in range(2):
                        j = oc * 2 + tsx
                        if j >= 2:
                            ctx.wait(t, "VS", vs_sg[(oc - 1, tsx)])
                        p = ps_s6[j % 2]
                        for kc in range(16):
                            ins = t.matmul(p[:], wobuf[oc % 2][:, kc * 128:(kc + 1) * 128],
                                           o_sb[:, kc, tsx * 512:(tsx + 1) * 512],
                                           start=(kc == 0), stop=(kc == 15))
                        ctx.inc(ins, "TS")
                for tt in range(NT_OWN):
                    for oc in range(16):
                        j6 = tt * 16 + oc
                        if j6 == 0:
                            ctx.wait(t, "VS", vs_sg_all)
                        if j6 >= 2:
                            pj = j6 - 2
                            ctx.wait(t, "VS", vs_add[pj // 16][pj % 16])
                        ctx.inc(t.transpose(ps_tr6[j6 % 2][:],
                                            g_sb[:, oc, tt * 128:(tt + 1) * 128],
                                            id_f[:]), "TS")

            def p6_scalar(sc):
                for oc in range(16):
                    for tsx in range(2):
                        j = oc * 2 + tsx
                        if j >= 2:
                            ctx.wait(sc, "VS", vs_sg[(oc - 1, tsx)])
                        ctx.inc(sc.activation(gsil[j % 2][:],
                                              g_sb[:, oc, tsx * 512:(tsx + 1) * 512],
                                              AF.Silu), "SS")

            def p6_vector(v):
                if _DEBUG:
                    DBG.wait_all(v)
                for oc in range(16):
                    for tsx in range(2):
                        j = oc * 2 + tsx
                        ctx.wait(v, "TS", ts_s[(oc, tsx)])
                        ctx.wait(v, "SS", ss_gs[(oc, tsx)])
                        ctx.inc(v.scalar_tensor_tensor(
                            g_sb[:, oc, tsx * 512:(tsx + 1) * 512],
                            ps_s6[j % 2][:], bo_pc[:, oc:oc + 1], gsil[j % 2][:],
                            OP.add, OP.mult), "VS")
                for tt in range(NT_OWN):
                    U2.wait(v, u2tok[tt])
                    for oc in range(16):
                        ctx.wait(v, "TS", ts_tp6[tt][oc])
                        if tt >= 2 and oc == 0:
                            OUT.wait(v, outtok[tt - 2])
                        ctx.inc(v.tensor_add(out_buf[tt % 2][:, oc * 128:(oc + 1) * 128],
                                             ps_tr6[(tt * 16 + oc) % 2][:],
                                             u2_buf[tt % 2][:, oc * 128:(oc + 1) * 128]),
                                "VS")

            with ExitStack() as sP6:
                ps_s6 = [sP6.enter_context(nc.psum_tensor(f"ps_s6{i}", [128, 512], F32))
                         for i in range(2)]
                ps_tr6 = [sP6.enter_context(nc.psum_tensor(f"ps_tr6{i}", [128, 128], F32))
                          for i in range(2)]
                block.sync(p6_sync)
                block.tensor(p6_tensor)
                block.scalar(p6_scalar)
                block.vector(p6_vector)
                assert ctx.cnt["TS"] == cTS and ctx.cnt["SS"] == cSS
                assert ctx.cnt["VS"] == cVS

            if _DEBUG:
                def dmp6(s):
                    DBG.dma(s, dbg["of"], d_oscr)
                    DBG.wait_all(s)
                block.sync(dmp6)

    return nc

def _prep_inputs(L0_attn_u, L0_attn_Wqkv, L0_attn_bqkv, L0_attn_Wo, L0_attn_bo,
                 L0_attn_sinks, L0_ffn_w_gate_up, L0_ffn_w_down):
    u = np.ascontiguousarray(L0_attn_u, dtype=np.float32)
    Wqkv = np.asarray(L0_attn_Wqkv, dtype=np.float32)
    bqkv = np.asarray(L0_attn_bqkv, dtype=np.float32)
    Wo = np.asarray(L0_attn_Wo, dtype=np.float32)
    bo = np.asarray(L0_attn_bo, dtype=np.float32)
    sinks = np.asarray(L0_attn_sinks, dtype=np.float32)
    Wgu = np.asarray(L0_ffn_w_gate_up, dtype=np.float32)
    Wd = np.asarray(L0_ffn_w_down, dtype=np.float32)

    A = Wqkv[:4096].reshape(32, 128, 16, 128)
    wqkv_pack = np.ascontiguousarray(A.transpose(0, 3, 2, 1)).reshape(32, 128, 2048)
    Bv = Wqkv[4096:].reshape(8, 256, 16, 128)
    v_pack = np.ascontiguousarray(Bv.transpose(0, 3, 2, 1)).reshape(8, 128, 4096)
    Ao = Wo.reshape(16, 128, 16, 128)
    wo_pack = np.ascontiguousarray(Ao.transpose(0, 3, 2, 1)).reshape(16, 128, 2048)
    G = Wgu.reshape(128, 128, 16, 128)
    pack_all = np.ascontiguousarray(G.transpose(0, 3, 2, 1)).reshape(128, 128, 2048)
    order = []
    for ib in range(8):
        for ic in range(8):
            order.append(ib * 8 + ic)
            order.append(64 + ib * 8 + ic)
    wgu_pack = np.ascontiguousarray(pack_all[order])
    wd_pack = np.ascontiguousarray(Wd.T.reshape(64, 128, 2048))

    bqkv_pc = np.ascontiguousarray(bqkv[:4096].reshape(32, 128).T)
    swp = np.concatenate([np.arange(32, 64), np.arange(0, 32),
                          np.arange(96, 128), np.arange(64, 96)])
    bqkv_sw = np.ascontiguousarray(bqkv_pc[swp, :])
    bqkv_pc48 = np.ascontiguousarray(bqkv.reshape(48, 128).T)
    vbias_row = np.ascontiguousarray(bqkv[None, 4096:])
    bo_pc = np.ascontiguousarray(bo.reshape(16, 128).T)
    sinks_t = np.ascontiguousarray(sinks.transpose(0, 2, 1))
    sinks_n = np.ascontiguousarray(sinks)
    ident = np.eye(128, dtype=np.float32)
    ones_row = np.ones((1, EXT), dtype=np.float32)
    consts = np.broadcast_to(np.array([1e-5, -1.0, 1.0 / 2048.0, 0.125], np.float32),
                             (128, 4)).copy()

    inv = (10000.0 ** (-np.arange(32, dtype=np.float64) / 32)).astype(np.float64)

    def rope_tables(pos, scale):
        ang = np.outer(inv, pos.astype(np.float64))
        C = np.tile(np.cos(ang), (4, 1)) * scale
        sn = np.sin(ang)
        S = np.concatenate([-sn, sn, -sn, sn], axis=0) * scale
        return (np.ascontiguousarray(C, dtype=np.float32),
                np.ascontiguousarray(S, dtype=np.float32))

    tabs = []
    masks_q = []
    nb = L // 64
    for q4 in range(4):
        pos_k = np.clip(q4 * TOK - HALO + np.arange(EXT), 0, L - 1)
        ck, sk = rope_tables(pos_k, 1.0)
        m = np.zeros((NT_OWN, 128, 272), np.float32)
        ii = np.arange(128)[:, None]
        jj = np.arange(256)[None, :]
        band = np.where((jj >= ii) & (jj <= ii + 128), 0.0, NEG).astype(np.float32)
        for qt in range(NT_OWN):
            mm = band.copy()
            n0 = q4 * 16 + qt * 2
            if n0 - 1 < 0:
                mm[:, 0:64] = NEG
            if n0 + 2 > nb - 1:
                mm[:, 192:256] = NEG
            m[qt, :, NS:272] = mm
        tabs.append((ck, sk))
        masks_q.append(m)

    in_maps = []
    for c in range(NCORE):
        b = c // 4
        q4 = c % 4
        start = q4 * TOK
        u_ext = np.zeros((EXT, D), np.float32)
        lo = max(0, start - HALO)
        hi = min(L, start + TOK + HALO)
        u_ext[lo - (start - HALO):hi - (start - HALO)] = u[b, lo:hi]
        ck, sk = tabs[q4]
        in_maps.append(dict(
            u_ext=u_ext, wqkv_pack=wqkv_pack, v_pack=v_pack, wo_pack=wo_pack,
            wgu_pack=wgu_pack, wd_pack=wd_pack, bqkv_pc=bqkv_pc48, bqkv_sw=bqkv_sw, bo_pc=bo_pc,
            sinks_t=sinks_t, sinks_n=sinks_n, ck=ck, sk=sk, vbias_row=vbias_row,
            masks=masks_q[q4], ident=ident, ones_row=ones_row, consts=consts,
        ))
    return in_maps


def kernel(L0_attn_u, L0_attn_Wqkv, L0_attn_bqkv, L0_attn_Wo, L0_attn_bo,
           L0_attn_sinks, L0_ffn_w_gate_up, L0_ffn_w_down):
    in_maps = _prep_inputs(L0_attn_u, L0_attn_Wqkv, L0_attn_bqkv, L0_attn_Wo,
                           L0_attn_bo, L0_attn_sinks, L0_ffn_w_gate_up,
                           L0_ffn_w_down)
    nc = build_program2()
    res = run_bass_kernel_spmd(nc, in_maps, core_ids=list(range(NCORE)))
    out = np.empty((B, L, D), np.float32)
    for c in range(NCORE):
        b = c // 4
        q4 = c % 4
        out[b, q4 * TOK:(q4 + 1) * TOK] = res.results[c]["out"]
    return out


# version 06
# speedup vs baseline: 1.0842x; 1.0842x over previous
import numpy as np
from contextlib import ExitStack

import concourse.bass as bass
import concourse.mybir as mybir
from concourse.bass_utils import run_bass_kernel_spmd

F32 = mybir.dt.float32
BF16 = mybir.dt.bfloat16
AX = mybir.AxisListType.X
AF = mybir.ActivationFunctionType
OP = mybir.AluOpType

B, L, D = 2, 4096, 2048
NCORE = 8
TOK = 1024
HALO = 64
EXT = TOK + 2 * HALO
NT_EXT = EXT // 128
NT_OWN = TOK // 128
DH = 64
NH = 32
NS = 16
NEG = -1.0e30

_DEBUG = False
_PHASES = 6


class Slots:

    def __init__(self, nc, es, name, n):
        self.sems = [es.enter_context(nc.semaphore(f"{name}{i}")) for i in range(n)]
        self.cnt = [0] * n

    def dma(self, q, out, in_, slot=0):
        ins = q.dma_start(out=out, in_=in_)
        ins.then_inc(self.sems[slot], 16)
        self.cnt[slot] += 16
        return (slot, self.cnt[slot])

    def wait(self, eng, tok):
        slot, val = tok
        eng.wait_ge(self.sems[slot], val)

    def wait_all(self, eng):
        for i, c in enumerate(self.cnt):
            if c > 0:
                eng.wait_ge(self.sems[i], c)


class Ctx:
    def __init__(self, nc, es):
        self.nc = nc
        self.cnt = {}
        self.sem = {}
        for name in ["TS", "VS", "SS"]:
            self.sem[name] = es.enter_context(nc.semaphore(name))
            self.cnt[name] = 0

    def inc(self, instr, name, n=1):
        instr.then_inc(self.sem[name], n)
        self.cnt[name] += n
        return self.cnt[name]

    def wait(self, eng, name, val):
        if val > 0:
            eng.wait_ge(self.sem[name], val)


def barrier(ctx, block):
    snap = dict(ctx.cnt)
    owners = {"TS": "tensor", "VS": "vector", "SS": "scalar"}

    def mk(engname):
        def f(eng):
            for s, v in snap.items():
                if v > 0 and owners[s] != engname:
                    eng.wait_ge(ctx.sem[s], v)
        return f
    block.tensor(mk("tensor"))
    block.vector(mk("vector"))
    block.scalar(mk("scalar"))
    block.gpsimd(mk("gpsimd"))
    block.sync(mk("sync"))


def build_program2():
    nc = bass.Bass("TRN2", target_bir_lowering=False)

    d_u = nc.dram_tensor("u_ext", [EXT, D], F32, kind="ExternalInput").ap()
    d_wqkv = nc.dram_tensor("wqkv_pack", [32, 128, 2048], F32, kind="ExternalInput").ap()
    d_vpk = nc.dram_tensor("v_pack", [8, 128, 4096], F32, kind="ExternalInput").ap()
    d_wo = nc.dram_tensor("wo_pack", [16, 128, 2048], F32, kind="ExternalInput").ap()
    d_wgu = nc.dram_tensor("wgu_pack", [128, 128, 2048], F32, kind="ExternalInput").ap()
    d_wd = nc.dram_tensor("wd_pack", [64, 128, 2048], F32, kind="ExternalInput").ap()
    d_bqkv_pc = nc.dram_tensor("bqkv_pc", [128, 48], F32, kind="ExternalInput").ap()
    d_bqkv_sw = nc.dram_tensor("bqkv_sw", [128, 32], F32, kind="ExternalInput").ap()
    d_vbias = nc.dram_tensor("vbias_row", [1, 2048], F32, kind="ExternalInput").ap()
    d_bo = nc.dram_tensor("bo_pc", [128, 16], F32, kind="ExternalInput").ap()
    d_sinkt = nc.dram_tensor("sinks_t", [NH, DH, NS], F32, kind="ExternalInput").ap()
    d_sinkn = nc.dram_tensor("sinks_n", [NH, NS, DH], F32, kind="ExternalInput").ap()
    d_ck = nc.dram_tensor("ck", [128, EXT], F32, kind="ExternalInput").ap()
    d_sk = nc.dram_tensor("sk", [128, EXT], F32, kind="ExternalInput").ap()
    d_masks = nc.dram_tensor("masks", [NT_OWN, 128, 272], F32, kind="ExternalInput").ap()
    d_ident = nc.dram_tensor("ident", [128, 128], F32, kind="ExternalInput").ap()
    d_ones = nc.dram_tensor("ones_row", [1, EXT], F32, kind="ExternalInput").ap()
    d_consts = nc.dram_tensor("consts", [128, 4], F32, kind="ExternalInput").ap()
    d_out = nc.dram_tensor("out", [TOK, D], F32, kind="ExternalOutput").ap()
    d_oscr = nc.dram_tensor("o_scr", [16, 128, 1024], BF16, kind="Internal").ap()

    dbg = {}
    if _DEBUG:
        dbg["xnf"] = nc.dram_tensor("d_xnf", [128, 16 * EXT], BF16, kind="ExternalOutput").ap()
        dbg["qf"] = nc.dram_tensor("d_qf", [128, 16 * TOK], BF16, kind="ExternalOutput").ap()
        dbg["kf"] = nc.dram_tensor("d_kf", [128, 16 * EXT], BF16, kind="ExternalOutput").ap()
        dbg["vt"] = nc.dram_tensor("d_vt", [128, NT_EXT * 2048], BF16, kind="ExternalOutput").ap()
        dbg["of"] = nc.dram_tensor("d_of", [16, 128, 1024], BF16, kind="ExternalOutput").ap()
        dbg["g"] = nc.dram_tensor("d_g", [128, 16 * TOK], F32, kind="ExternalOutput").ap()

    with ExitStack() as glob:
        eg = glob.enter_context
        ctx = Ctx(nc, glob)
        DBG = Slots(nc, glob, "dbg", 1) if _DEBUG else None

        xn_f = eg(nc.sbuf_tensor("xn_f", [128, 16, EXT], BF16))
        consts = eg(nc.sbuf_tensor("consts_sb", [128, 4], F32))
        bo_pc = eg(nc.sbuf_tensor("bo_sb", [128, 16], F32))
        id_f = eg(nc.sbuf_tensor("id_f", [128, 128], F32))
        id_bf = eg(nc.sbuf_tensor("id_bf", [128, 128], BF16))
        sumv = eg(nc.sbuf_tensor("sumv", [128, 1], F32))
        sumsq = eg(nc.sbuf_tensor("sumsq", [128, 1], F32))
        meanv = eg(nc.sbuf_tensor("meanv", [128, 1], F32))
        msq = eg(nc.sbuf_tensor("msq", [128, 1], F32))
        ex2 = eg(nc.sbuf_tensor("ex2", [128, 1], F32))
        varep = eg(nc.sbuf_tensor("varep", [128, 1], F32))
        sqv = eg(nc.sbuf_tensor("sqv", [128, 1], F32))
        rstd = eg(nc.sbuf_tensor("rstd", [128, 1], F32))
        tmp1 = eg(nc.sbuf_tensor("tmp1", [128, 1], F32))
        negmr = eg(nc.sbuf_tensor("negmr", [128, 1], F32))

        block = eg(nc.Block())

        CS = Slots(nc, glob, "cs", 1)
        CG = Slots(nc, glob, "cg", 1)

        def p0_sync(s):
            CS.dma(s, id_f[:], d_ident)
            CS.dma(s, consts[:], d_consts)
            CS.dma(s, bo_pc[:], d_bo)

        def p0_gp(g):
            CG.dma(g, id_bf[:], d_ident)
        block.sync(p0_sync)
        block.gpsimd(p0_gp)
        for emit in (block.tensor, block.vector, block.scalar):
            def f(eng):
                CS.wait_all(eng)
                CG.wait_all(eng)
            emit(f)

        sP1mem = glob.enter_context(ExitStack())
        u_buf = [sP1mem.enter_context(nc.sbuf_tensor(f"u_buf{i}", [128, D], F32))
                 for i in range(2)]
        xn_bf = [sP1mem.enter_context(nc.sbuf_tensor(f"xn_bf{i}", [128, D], BF16))
                 for i in range(2)]
        sq_scr = sP1mem.enter_context(nc.sbuf_tensor("sq_scr", [128, D], BF16))
        U = Slots(nc, glob, "u", 2)

        base = dict(ctx.cnt)
        utok = [None] * NT_EXT
        ss_sq = [0] * NT_EXT
        ss_sqrt = [0] * NT_EXT
        vs_stat = [0] * NT_EXT
        vs_fin = [0] * NT_EXT
        ss_xn = [0] * NT_EXT
        ts_tr = [[0] * 16 for _ in range(NT_EXT)]
        vs_ev = [[0] * 16 for _ in range(NT_EXT)]
        cSS, cVS, cTS = base["SS"], base["VS"], base["TS"]
        ucnt = [0, 0]
        for tt in range(NT_EXT):
            ucnt[tt % 2] += 16
            utok[tt] = (tt % 2, ucnt[tt % 2])
            cSS += 1; ss_sq[tt] = cSS
            cVS += 1; vs_stat[tt] = cVS
            cSS += 1; ss_sqrt[tt] = cSS
            cVS += 1; vs_fin[tt] = cVS
            cSS += 1; ss_xn[tt] = cSS
            for kc in range(16):
                cTS += 1; ts_tr[tt][kc] = cTS
                cVS += 1; vs_ev[tt][kc] = cVS

        def p1_sync(s):
            for tt in range(NT_EXT):
                if tt >= 2:
                    ctx.wait(s, "SS", ss_xn[tt - 2])
                tok = U.dma(s, u_buf[tt % 2][:], d_u[tt * 128:(tt + 1) * 128, :],
                            tt % 2)
                assert tok == utok[tt]
        block.sync(p1_sync)

        def p1_scalar(sc):
            for tt in range(NT_EXT):
                U.wait(sc, utok[tt])
                ctx.inc(sc.activation(sq_scr[:], u_buf[tt % 2][:], AF.Square,
                                      accum_out=sumsq[:]), "SS")
                ctx.wait(sc, "VS", vs_stat[tt])
                ctx.inc(sc.activation(sqv[:], varep[:], AF.Sqrt), "SS")
                ctx.wait(sc, "VS", vs_fin[tt])
                if tt >= 2:
                    ctx.wait(sc, "TS", ts_tr[tt - 2][15])
                ctx.inc(sc.activation(xn_bf[tt % 2][:], u_buf[tt % 2][:], AF.Identity,
                                      bias=negmr[:], scale=rstd[:]), "SS")
        block.scalar(p1_scalar)

        def p1_tensor(t):
            for tt in range(NT_EXT):
                ctx.wait(t, "SS", ss_xn[tt])
                for kc in range(16):
                    if tt * 16 + kc >= 2:
                        prev = tt * 16 + kc - 2
                        ctx.wait(t, "VS", vs_ev[prev // 16][prev % 16])
                    ctx.inc(t.transpose(ps_tr[kc % 2][:],
                                        xn_bf[tt % 2][:, kc * 128:(kc + 1) * 128],
                                        id_bf[:]), "TS")

        def p1_vector(v):
            for tt in range(NT_EXT):
                U.wait(v, utok[tt])
                ub = u_buf[tt % 2]
                v.reduce_sum(sumv[:], ub[:], axis=AX)
                v.drain()
                v.tensor_mul(meanv[:], sumv[:], consts[:, 2:3])
                v.drain()
                v.tensor_mul(msq[:], meanv[:], meanv[:])
                ctx.wait(v, "SS", ss_sq[tt])
                v.tensor_mul(ex2[:], sumsq[:], consts[:, 2:3])
                v.drain()
                v.tensor_sub(tmp1[:], ex2[:], msq[:])
                v.drain()
                ctx.inc(v.tensor_add(varep[:], tmp1[:], consts[:, 0:1]), "VS")
                ctx.wait(v, "SS", ss_sqrt[tt])
                v.reciprocal(rstd[:], sqv[:])
                v.drain()
                v.tensor_mul(tmp1[:], meanv[:], rstd[:])
                v.drain()
                ctx.inc(v.tensor_mul(negmr[:], tmp1[:], consts[:, 1:2]), "VS")
                for kc in range(16):
                    ctx.wait(v, "TS", ts_tr[tt][kc])
                    ctx.inc(v.tensor_copy(xn_f[:, kc, tt * 128:(tt + 1) * 128],
                                          ps_tr[kc % 2][:]), "VS")
        with ExitStack() as sP1:
            ps_tr = [sP1.enter_context(nc.psum_tensor(f"ps_tr{i}", [128, 128], BF16))
                     for i in range(2)]
            block.tensor(p1_tensor)
            block.vector(p1_vector)
        assert ctx.cnt["SS"] == cSS and ctx.cnt["VS"] == cVS and ctx.cnt["TS"] == cTS
        barrier(ctx, block)

        if _DEBUG:
            def dmp(s):
                DBG.dma(s, dbg["xnf"], xn_f[:].rearrange("p a b -> p (a b)"))
                DBG.wait_all(s)
            block.sync(dmp)
            for emitf in (block.tensor, block.vector, block.scalar, block.gpsimd):
                def fdbg1(eng):
                    DBG.wait_all(eng)
                emitf(fdbg1)
            barrier(ctx, block)
        if _PHASES <= 1:
            return nc

        sP1mem.close()
        OS_outer = [None]
        with ExitStack() as sA:
            ea = sA.enter_context
            q_f = ea(nc.sbuf_tensor("q_f", [128, 16, TOK], BF16))
            k_f = ea(nc.sbuf_tensor("k_f", [128, 16, EXT], BF16))
            v_t = ea(nc.sbuf_tensor("v_t", [128, NT_EXT, 2048], BF16))
            ck_t = ea(nc.sbuf_tensor("ck_t", [128, EXT], F32))
            sk_t = ea(nc.sbuf_tensor("sk_t", [128, EXT], F32))
            wx = [ea(nc.sbuf_tensor(f"wx{i}", [128, 4096], BF16)) for i in range(3)]
            t1 = ea(nc.sbuf_tensor("t1", [128, 512], F32))
            m2 = ea(nc.sbuf_tensor("m2", [128, 512], F32))
            t2sw = [ea(nc.sbuf_tensor(f"t2sw{i}", [128, 512], F32)) for i in range(2)]
            ones_bf = ea(nc.sbuf_tensor("ones_bf", [1, EXT], BF16))
            bq_pc = ea(nc.sbuf_tensor("bq_pc", [128, 48], F32))
            bq_sw = ea(nc.sbuf_tensor("bq_sw", [128, 32], F32))
            vbias_bf = ea(nc.sbuf_tensor("vbias_bf", [1, 2048], BF16))
            sinkt_bf = ea(nc.sbuf_tensor("sinkt_bf", [128, NH * NS], BF16))
            sinkn_bf = ea(nc.sbuf_tensor("sinkn_bf", [NS, NH * DH], BF16))
            masks_bf = ea(nc.sbuf_tensor("masks_bf", [128, NT_OWN, 272], BF16))
            o_buf = [ea(nc.sbuf_tensor(f"o_buf{i}", [128, 1024], BF16)) for i in range(2)]
            pA_bf = [ea(nc.sbuf_tensor(f"pA_bf{i}", [128, 272], BF16)) for i in range(2)]
            pB_bf = [ea(nc.sbuf_tensor(f"pB_bf{i}", [128, 272], BF16)) for i in range(2)]
            pt_sb = [ea(nc.sbuf_tensor(f"pt_sb{i}", [128, 512], BF16)) for i in range(2)]
            pts_sb = [ea(nc.sbuf_tensor(f"pts_sb{i}", [NS, 256], BF16)) for i in range(2)]
            nmA = [ea(nc.sbuf_tensor(f"nmA{i}", [128, 1], F32)) for i in range(2)]
            nmB = [ea(nc.sbuf_tensor(f"nmB{i}", [128, 1], F32)) for i in range(2)]
            n2A = [ea(nc.sbuf_tensor(f"n2A{i}", [128, 1], F32)) for i in range(2)]
            n2B = [ea(nc.sbuf_tensor(f"n2B{i}", [128, 1], F32)) for i in range(2)]
            rsA = [ea(nc.sbuf_tensor(f"rsA{i}", [128, 1], F32)) for i in range(2)]
            rsB = [ea(nc.sbuf_tensor(f"rsB{i}", [128, 1], F32)) for i in range(2)]
            rcA = [ea(nc.sbuf_tensor(f"rcA{i}", [128, 1], F32)) for i in range(2)]
            rcB = [ea(nc.sbuf_tensor(f"rcB{i}", [128, 1], F32)) for i in range(2)]

            TB = Slots(nc, glob, "tb", 1)
            CB = Slots(nc, glob, "cb", 1)
            MS = Slots(nc, glob, "ms", 1)
            WX = Slots(nc, glob, "wxs", 3)
            OS = Slots(nc, glob, "os", 2)
            OS_outer[0] = OS

            def p2_consts(g):
                CB.dma(g, ones_bf[:], d_ones)
                CB.dma(g, vbias_bf[:], d_vbias)
                MS.dma(g, sinkt_bf[0:64, :].rearrange("d (h s) -> d h s", h=NH),
                       d_sinkt.rearrange("h d s -> d h s"))
                MS.dma(g, sinkt_bf[64:128, :].rearrange("d (h s) -> d h s", h=NH),
                       d_sinkt.rearrange("h d s -> d h s"))
                MS.dma(g, sinkn_bf[:].rearrange("s (h d) -> s h d", h=NH),
                       d_sinkn.rearrange("h s d -> s h d"))
                MS.dma(g, masks_bf[:], d_masks.rearrange("q p c -> p q c"))
            block.gpsimd(p2_consts)

            def p2_tables(s):
                TB.dma(s, ck_t[:], d_ck)
                TB.dma(s, sk_t[:], d_sk)
                TB.dma(s, bq_pc[:], d_bqkv_pc)
                TB.dma(s, bq_sw[:], d_bqkv_sw)
            block.sync(p2_tables)

            q_ts = [(0, 512), (512, 512)]
            k_ts = [(0, 512), (512, 512), (1024, 128)]

            loads = []
            units = []
            unit_of_step = [[] for _ in range(16)]
            for c in range(16):
                loads.append(("q", c))
                li_q = len(loads) - 1
                for tsi, (off, ln) in enumerate(q_ts):
                    unit_of_step[c].append(("qk", c, tsi, off, ln, li_q))
                loads.append(("k", 16 + c))
                li_k = len(loads) - 1
                for tsi, (off, ln) in enumerate(k_ts):
                    unit_of_step[c].append(("qk", 16 + c, tsi, off, ln, li_k))
                if c % 2 == 0:
                    loads.append(("v", c // 2))
                    li_v = len(loads) - 1
                    for tt in range(NT_EXT):
                        unit_of_step[c].append(("v", c // 2, tt, li_v))
            for c in range(16):
                units.extend(unit_of_step[c])
            NU = len(units)
            unit_idx = {u: i for i, u in enumerate(units)}
            last_unit_of_load = {}
            for i, u in enumerate(units):
                last_unit_of_load[u[-1]] = i

            pairs = [(c, qt) for c in range(16) for qt in range(NT_OWN)]
            NP3 = len(pairs)
            rope_done_of_chunk = [0] * 16

            base = dict(ctx.cnt)
            cTS, cVS, cSS = base["TS"], base["VS"], base["SS"]
            wxtok = [None] * len(loads)
            ts_mm = [0] * NU
            vs_t1 = [0] * NU
            ss_sw = [0] * NU
            vs_rope = [0] * NU
            ts_sc = [0] * NP3
            ts_tp = [0] * NP3
            ts_av = [0] * NP3
            vs_mx = [0] * NP3
            vs_nm = [0] * NP3
            vs_pt = [0] * NP3
            ss_ex = [0] * NP3
            ss_oe = [0] * NP3
            ostok = [None] * 16
            oc_ = [0, 0]
            wxc = [0, 0, 0]
            for li in range(len(loads)):
                sl = li % 3
                wxc[sl] += 16
                wxtok[li] = (sl, wxc[sl])
            for c in range(16):
                oc_[c % 2] += 16
                ostok[c] = (c % 2, oc_[c % 2])

            def p3_window(c):
                if c == 0:
                    return []
                return list(range((c - 1) * NT_OWN, c * NT_OWN))

            def step_items(c):
                U = unit_of_step[c] if c < 16 else []
                P = p3_window(c)
                items = []
                for k in range(max(len(U), len(P))):
                    if k < len(U):
                        items.append(("u", unit_idx[U[k]]))
                    if k < len(P):
                        items.append(("p", P[k]))
                return items, P

            tsched = []
            vsched = []
            ssched = []
            for c in range(17):
                items, P = step_items(c)
                w0 = P[0] if P else 0
                for kind, x in items:
                    if kind == "u":
                        tsched.append(("u", x))
                        if units[x][0] == "qk":
                            vsched.append(("u", x))
                        ssched.append(("u", x))
                    else:
                        j = x
                        tsched.append(("sc", j))
                        if j - 1 >= w0:
                            tsched.append(("tp", j - 1))
                        if j - 2 >= w0:
                            tsched.append(("av", j - 2))
                        vsched.append(("mx", j))
                        vsched.append(("nm", j))
                        if j - 1 >= w0:
                            vsched.append(("ev", j - 1))
                        ssched.append(("ex", j))
                        if j - 2 >= w0:
                            ssched.append(("oe", j - 2))
                if P:
                    w1 = P[-1] + 1
                    tsched.append(("av", w1 - 2))
                    tsched.append(("tp", w1 - 1))
                    tsched.append(("av", w1 - 1))
                    vsched.append(("ev", w1 - 1))
                    ssched.append(("oe", w1 - 2))
                    ssched.append(("oe", w1 - 1))

            for kind, x in tsched:
                cTS += 1
                if kind == "u":
                    ts_mm[x] = cTS
                elif kind == "sc":
                    ts_sc[x] = cTS
                elif kind == "tp":
                    ts_tp[x] = cTS
                else:
                    ts_av[x] = cTS
            for kind, x in vsched:
                if kind == "u":
                    cVS += 1; vs_t1[x] = cVS
                    cVS += 1; vs_rope[x] = cVS
                    u = units[x]
                    if u[0] == "qk" and u[1] >= 16:
                        rope_done_of_chunk[u[1] - 16] = cVS
                elif kind == "mx":
                    cVS += 1; vs_mx[x] = cVS
                elif kind == "nm":
                    cVS += 1; vs_nm[x] = cVS
                else:
                    cVS += 1; vs_pt[x] = cVS
            for i_, u in enumerate(units):
                if u[0] == "v":
                    pass
            run = 0
            vs_after_unit = [0] * NU
            for i_, u in enumerate(units):
                if u[0] == "v":
                    vs_t1[i_] = 0
                    vs_rope[i_] = 0
            for kind, x in ssched:
                cSS += 1
                if kind == "u":
                    ss_sw[x] = cSS
                elif kind == "ex":
                    ss_ex[x] = cSS
                else:
                    ss_oe[x] = cSS

            def pm_gp(g):
                for li, ld in enumerate(loads):
                    if li >= 3:
                        ctx.wait(g, "TS", ts_mm[last_unit_of_load[li - 3]])
                    kind = ld[0]
                    if kind == "q":
                        src = d_wqkv[ld[1]]
                        tok = WX.dma(g, wx[li % 3][:, 0:2048], src, li % 3)
                    elif kind == "k":
                        src = d_wqkv[ld[1]]
                        tok = WX.dma(g, wx[li % 3][:, 0:2048], src, li % 3)
                    else:
                        src = d_vpk[ld[1]]
                        tok = WX.dma(g, wx[li % 3][:], src, li % 3)
                    assert tok == wxtok[li]
            block.gpsimd(pm_gp)

            def emit_unit_mm(t, u):
                i = unit_idx[u]
                if u[0] == "qk":
                    _, oc, tsi, off, ln, li = u
                    if tsi == 0:
                        WX.wait(t, wxtok[li])
                    if i >= 2:
                        ctx.wait(t, "VS", vs_rope[i - 2])
                        ctx.wait(t, "SS", ss_sw[i - 2])
                    p = ps_qk[i % 2]
                    if oc < 16:
                        src = xn_f[:, :, HALO + off:HALO + off + ln]
                    else:
                        src = xn_f[:, :, off:off + ln]
                    w = wx[li % 3]
                    for kc in range(16):
                        ins = t.matmul(p[:, 0:ln], w[:, kc * 128:(kc + 1) * 128],
                                       src[:, kc, :], start=(kc == 0), stop=(kc == 15))
                    ctx.inc(ins, "TS")
                else:
                    _, gp, tt, li = u
                    if tt == 0:
                        WX.wait(t, wxtok[li])
                    if i >= 2:
                        ctx.wait(t, "VS", vs_rope[i - 2])
                        ctx.wait(t, "SS", ss_sw[i - 2])
                    p = ps_qk[i % 2]
                    w = wx[li % 3]
                    for kc in range(16):
                        t.matmul(p[:, 0:256], xn_f[:, kc, tt * 128:(tt + 1) * 128],
                                 w[:, kc * 256:(kc + 1) * 256],
                                 start=(kc == 0), stop=False)
                    ctx.inc(t.matmul(p[:, 0:256], ones_bf[0:1, 0:128],
                                     vbias_bf[0:1, gp * 256:(gp + 1) * 256],
                                     start=False, stop=True), "TS")

            def p3_emit_scores(t, j):
                c, qt = pairs[j]
                if j >= 2:
                    ctx.wait(t, "SS", ss_ex[j - 2])
                if qt == 0:
                    ctx.wait(t, "VS", rope_done_of_chunk[c])
                for hh in range(2):
                    h = 2 * c + hh
                    po = hh * 64
                    qsl = q_f[po:po + 64, c, qt * 128:(qt + 1) * 128]
                    ksl = k_f[po:po + 64, c, qt * 128:qt * 128 + 256]
                    ps = (psA if hh == 0 else psB)[j % 2]
                    t.matmul(ps[:], id_bf[:], masks_bf[:, qt, :], start=True, stop=False)
                    t.matmul(ps[:, 0:NS], qsl, sinkt_bf[po:po + 64, h * NS:(h + 1) * NS],
                             start=False, stop=False)
                    ins = t.matmul(ps[:, NS:272], qsl, ksl, start=False, stop=True)
                ctx.inc(ins, "TS")

            def p3_emit_tp(t, i):
                ctx.wait(t, "VS", vs_nm[i])
                if i >= 1:
                    ctx.wait(t, "VS", vs_pt[i - 1])
                pt = ps_tp
                pa, pb = pA_bf[i % 2], pB_bf[i % 2]
                t.transpose(pt[:, 0:128], pa[:, NS:NS + 128], id_bf[:])
                t.transpose(pt[:, 128:256], pa[:, NS + 128:NS + 256], id_bf[:])
                t.transpose(pt[0:NS, 256:384], pa[:, 0:NS], id_bf[:])
                t.transpose(pt[:, 384:512], pb[:, NS:NS + 128], id_bf[:])
                t.transpose(pt[:, 512:640], pb[:, NS + 128:NS + 256], id_bf[:])
                ins = t.transpose(pt[0:NS, 640:768], pb[:, 0:NS], id_bf[:])
                ctx.inc(ins, "TS")

            def p3_emit_av(t, i):
                c, qt = pairs[i]
                ctx.wait(t, "VS", vs_pt[i])
                if i >= 1:
                    ctx.wait(t, "SS", ss_oe[i - 1])
                vu = [u for u in unit_of_step[2 * (c // 2)] if u[0] == "v"][-1]
                ctx.wait(t, "SS", ss_sw[unit_idx[vu]])
                for hh in range(2):
                    h = 2 * c + hh
                    co = hh * 256
                    tp_ = (0, 0) if hh == 0 else (0, 64)
                    outp = ps_o[0:64, :] if hh == 0 else ps_o[64:128, :]
                    t.matmul(outp, v_t[:, qt, h * 64:(h + 1) * 64],
                             pt_sb[i % 2][:, co:co + 128], start=True, stop=False,
                             tile_position=tp_)
                    t.matmul(outp, v_t[:, qt + 1, h * 64:(h + 1) * 64],
                             pt_sb[i % 2][:, co + 128:co + 256], start=False, stop=False,
                             tile_position=tp_)
                    ins = t.matmul(outp, sinkn_bf[:, h * 64:(h + 1) * 64],
                                   pts_sb[i % 2][:, hh * 128:hh * 128 + 128],
                                   start=False, stop=True, tile_position=tp_)
                ctx.inc(ins, "TS")

            def pm_tensor(t):
                CB.wait_all(t)
                MS.wait_all(t)
                for kind, x in tsched:
                    if kind == "u":
                        emit_unit_mm(t, units[x])
                    elif kind == "sc":
                        p3_emit_scores(t, x)
                    elif kind == "tp":
                        p3_emit_tp(t, x)
                    else:
                        p3_emit_av(t, x)

            def pm_vector(v):
                TB.wait_all(v)
                for kind, x in vsched:
                    if kind == "u":
                        u = units[x]
                        i = x
                        _, oc, tsi, off, ln, li = u
                        p = ps_qk[i % 2]
                        sw = t2sw[i % 2]
                        ctx.wait(v, "TS", ts_mm[i])
                        if oc < 16:
                            cs, ss_ = (ck_t[:, HALO + off:HALO + off + ln],
                                       sk_t[:, HALO + off:HALO + off + ln])
                            dst = q_f[:, oc, off:off + ln]
                        else:
                            cs, ss_ = (ck_t[:, off:off + ln],
                                       sk_t[:, off:off + ln])
                            dst = k_f[:, oc - 16, off:off + ln]
                        ctx.inc(v.scalar_tensor_tensor(
                            t1[:, 0:ln], p[:, 0:ln], bq_pc[:, oc:oc + 1], cs,
                            OP.add, OP.mult), "VS")
                        ctx.wait(v, "SS", ss_sw[i])
                        v.scalar_tensor_tensor(
                            m2[:, 0:ln], sw[:, 0:ln], bq_sw[:, oc:oc + 1], ss_,
                            OP.add, OP.mult)
                        v.drain()
                        ctx.inc(v.tensor_add(dst, t1[:, 0:ln], m2[:, 0:ln]), "VS")
                        v.drain()
                    elif kind == "mx":
                        j = x
                        ctx.wait(v, "TS", ts_sc[j])
                        v.reduce_max(nmA[j % 2][:], psA[j % 2][:], axis=AX, negate=True)
                        v.reduce_max(nmB[j % 2][:], psB[j % 2][:], axis=AX, negate=True)
                        v.drain()
                        v.tensor_mul(n2A[j % 2][:], nmA[j % 2][:], consts[:, 3:4])
                        ctx.inc(v.tensor_mul(n2B[j % 2][:], nmB[j % 2][:],
                                             consts[:, 3:4]), "VS")
                    elif kind == "nm":
                        j = x
                        ctx.wait(v, "SS", ss_ex[j])
                        v.reciprocal(rcA[j % 2][:], rsA[j % 2][:])
                        v.reciprocal(rcB[j % 2][:], rsB[j % 2][:])
                        v.drain()
                        v.tensor_scalar_mul(pA_bf[j % 2][:], pA_bf[j % 2][:],
                                            rcA[j % 2][:])
                        ctx.inc(v.tensor_scalar_mul(pB_bf[j % 2][:], pB_bf[j % 2][:],
                                                    rcB[j % 2][:]), "VS")
                    else:
                        j = x
                        ctx.wait(v, "TS", ts_tp[j])
                        v.tensor_copy(pt_sb[j % 2][:, 0:256], ps_tp[:, 0:256])
                        v.tensor_copy(pt_sb[j % 2][:, 256:512], ps_tp[:, 384:640])
                        v.tensor_copy(pts_sb[j % 2][:, 0:128], ps_tp[0:NS, 256:384])
                        ctx.inc(v.tensor_copy(pts_sb[j % 2][:, 128:256],
                                              ps_tp[0:NS, 640:768]), "VS")

            def pm_scalar(sc):
                def emit_oevac(i):
                    c, qt = pairs[i]
                    ctx.wait(sc, "TS", ts_av[i])
                    if qt == 0 and c >= 2:
                        OS.wait(sc, ostok[c - 2])
                    sc.activation(o_buf[c % 2][0:64, qt * 128:(qt + 1) * 128],
                                  ps_o[0:64, :], AF.Copy)
                    ctx.inc(sc.activation(o_buf[c % 2][64:128,
                                                     qt * 128:(qt + 1) * 128],
                                          ps_o[64:128, :], AF.Copy), "SS")
                for kind, x in ssched:
                    if kind == "u":
                        u = units[x]
                        i = x
                        p = ps_qk[i % 2]
                        if u[0] == "qk":
                            sw = t2sw[i % 2]
                            ctx.wait(sc, "VS", vs_t1[i])
                            if i >= 2:
                                ctx.wait(sc, "VS", vs_rope[i - 2])
                            _, oc, tsi, off, ln, li = u
                            sc.copy(sw[0:32, 0:ln], p[32:64, 0:ln])
                            sc.copy(sw[32:64, 0:ln], p[0:32, 0:ln])
                            sc.copy(sw[64:96, 0:ln], p[96:128, 0:ln])
                            ctx.inc(sc.copy(sw[96:128, 0:ln], p[64:96, 0:ln]), "SS")
                        else:
                            _, gp, tt, li = u
                            ctx.wait(sc, "TS", ts_mm[i])
                            ctx.inc(sc.activation(v_t[:, tt, gp * 256:(gp + 1) * 256],
                                                  p[:, 0:256], AF.Copy), "SS")
                    elif kind == "ex":
                        j = x
                        ctx.wait(sc, "VS", vs_mx[j])
                        if j >= 2:
                            ctx.wait(sc, "TS", ts_tp[j - 2])
                        sc.activation(pA_bf[j % 2][:], psA[j % 2][:], AF.Exp,
                                      bias=n2A[j % 2][:], scale=consts[:, 3:4],
                                      accum_out=rsA[j % 2][:])
                        ctx.inc(sc.activation(pB_bf[j % 2][:], psB[j % 2][:],
                                              AF.Exp, bias=n2B[j % 2][:],
                                              scale=consts[:, 3:4],
                                              accum_out=rsB[j % 2][:]), "SS")
                    else:
                        emit_oevac(x)

            def pm_sync(s):
                for c in range(16):
                    last_i = c * NT_OWN + (NT_OWN - 1)
                    ctx.wait(s, "SS", ss_oe[last_i])
                    tok = OS.dma(s, d_oscr[c], o_buf[c % 2][:], c % 2)
                    assert tok == ostok[c]

            with ExitStack() as sPM:
                ps_qk = [sPM.enter_context(nc.psum_tensor(f"ps_qk{i}", [128, 512], F32))
                         for i in range(2)]
                psA = [sPM.enter_context(nc.psum_tensor(f"psA{i}", [128, 272], F32))
                       for i in range(2)]
                psB = [sPM.enter_context(nc.psum_tensor(f"psB{i}", [128, 272], F32))
                       for i in range(2)]
                ps_tp = sPM.enter_context(nc.psum_tensor("ps_tp", [128, 768], BF16))
                ps_o = sPM.enter_context(nc.psum_tensor("ps_o", [128, 128], F32))
                block.tensor(pm_tensor)
                block.vector(pm_vector)
                block.scalar(pm_scalar)
                block.sync(pm_sync)
                assert ctx.cnt["TS"] == cTS and ctx.cnt["VS"] == cVS
                assert ctx.cnt["SS"] == cSS
            barrier(ctx, block)
            if _DEBUG:
                def dmp2(s):
                    DBG.dma(s, dbg["qf"], q_f[:].rearrange("p a b -> p (a b)"))
                    DBG.dma(s, dbg["kf"], k_f[:].rearrange("p a b -> p (a b)"))
                    DBG.dma(s, dbg["vt"], v_t[:].rearrange("p a b -> p (a b)"))
                    OS.wait_all(s)
                    DBG.dma(s, dbg["of"], d_oscr)
                    DBG.wait_all(s)
                block.sync(dmp2)
                for emitf in (block.tensor, block.vector, block.scalar, block.gpsimd):
                    def fdbg(eng):
                        DBG.wait_all(eng)
                    emitf(fdbg)
                barrier(ctx, block)
        if _PHASES <= 3:
            return nc

        sB = glob.enter_context(ExitStack())
        g_sb = sB.enter_context(nc.sbuf_tensor("g_sb", [128, 16, TOK], F32))
        with ExitStack() as sC:
            ec = sC.enter_context
            h_ib = ec(nc.sbuf_tensor("h_ib", [128, 8, TOK], BF16))
            wd_ib = ec(nc.sbuf_tensor("wd_ib", [128, 8, 2048], BF16))
            gbuf = [ec(nc.sbuf_tensor(f"gbuf{i}", [128, 2048], BF16)) for i in range(2)]
            ubuf = [ec(nc.sbuf_tensor(f"ubuf{i}", [128, 2048], BF16)) for i in range(2)]
            tg = [ec(nc.sbuf_tensor(f"tg{i}", [128, 512], F32)) for i in range(2)]
            G = Slots(nc, glob, "gg", 2)
            UU = Slots(nc, glob, "uu", 2)
            WD = Slots(nc, glob, "wd", 1)

            base = dict(ctx.cnt)
            gtok = [[None] * 8 for _ in range(8)]
            utok2 = [[None] * 8 for _ in range(8)]
            wdtok = [None] * 8
            ts_gt = {}
            ts_up = {}
            ss_si = {}
            vs_h = {}
            ts_dn = {}
            vs_ge = {}
            cTS, cSS, cVS = base["TS"], base["SS"], base["VS"]
            gc_ = [0, 0]
            uc_ = [0, 0]
            wdc = 0
            for ib in range(8):
                for ic in range(8):
                    gc_[ic % 2] += 16; gtok[ib][ic] = (ic % 2, gc_[ic % 2])
                    uc_[ic % 2] += 16; utok2[ib][ic] = (ic % 2, uc_[ic % 2])
                wdc += 16 * 8; wdtok[ib] = (0, wdc)
                for ic in range(8):
                    for tsx in range(2):
                        cTS += 1; ts_gt[(ib, ic, tsx)] = cTS
                        cSS += 1; ss_si[(ib, ic, tsx)] = cSS
                        cTS += 1; ts_up[(ib, ic, tsx)] = cTS
                        cVS += 1; vs_h[(ib, ic, tsx)] = cVS
                for tsx in range(2):
                    for oc in range(16):
                        cTS += 1; ts_dn[(ib, tsx, oc)] = cTS
                        cVS += 1; vs_ge[(ib, tsx, oc)] = cVS

            ts_keys_g = [(ib, ic, tsx) for ib in range(8) for ic in range(8)
                         for tsx in range(2)]
            dn_keys = [(ib, tsx, oc) for ib in range(8) for tsx in range(2)
                       for oc in range(16)]

            def p5_gp(g):
                OS_outer[0].wait_all(g)
                for ib in range(8):
                    for ic in range(8):
                        pic = ib * 8 + ic - 2
                        if pic >= 0:
                            ctx.wait(g, "TS", ts_gt[(pic // 8, pic % 8, 1)])
                        tok = G.dma(g, gbuf[ic % 2][:], d_wgu[ib * 16 + ic * 2], ic % 2)
                        assert tok == gtok[ib][ic]
                        if pic >= 0:
                            ctx.wait(g, "TS", ts_up[(pic // 8, pic % 8, 1)])
                        tok = UU.dma(g, ubuf[ic % 2][:], d_wgu[ib * 16 + ic * 2 + 1],
                                     ic % 2)
                        assert tok == utok2[ib][ic]
                    if ib >= 1:
                        ctx.wait(g, "TS", ts_dn[(ib - 1, 1, 15)])
                    for ic in range(8):
                        tok = WD.dma(g, wd_ib[:, ic, :], d_wd[ib * 8 + ic], 0)
                    assert tok == wdtok[ib]
            block.gpsimd(p5_gp)

            def p5_tensor(t):
                for ib in range(8):
                    for ic in range(8):
                        for tsx in range(2):
                            gi = (ib * 8 + ic) * 2 + tsx
                            if tsx == 0:
                                G.wait(t, gtok[ib][ic])
                            if gi >= 2:
                                pk = ts_keys_g[gi - 2]
                                ctx.wait(t, "SS", ss_si[pk])
                            pg = ps_g[gi % 2]
                            for kc in range(16):
                                ins = t.matmul(
                                    pg[:], gbuf[ic % 2][:, kc * 128:(kc + 1) * 128],
                                    xn_f[:, kc, HALO + tsx * 512:HALO + (tsx + 1) * 512],
                                    start=(kc == 0), stop=(kc == 15))
                            ctx.inc(ins, "TS")
                            if tsx == 0:
                                UU.wait(t, utok2[ib][ic])
                            if gi >= 2:
                                pk = ts_keys_g[gi - 2]
                                ctx.wait(t, "VS", vs_h[pk])
                            pu = ps_u[gi % 2]
                            for kc in range(16):
                                ins = t.matmul(
                                    pu[:], ubuf[ic % 2][:, kc * 128:(kc + 1) * 128],
                                    xn_f[:, kc, HALO + tsx * 512:HALO + (tsx + 1) * 512],
                                    start=(kc == 0), stop=(kc == 15))
                            ctx.inc(ins, "TS")
                    WD.wait(t, wdtok[ib])
                    for tsx in range(2):
                        for oc in range(16):
                            j = (ib * 2 + tsx) * 16 + oc
                            if j >= 2:
                                ctx.wait(t, "VS", vs_ge[dn_keys[j - 2]])
                            ctx.wait(t, "VS", vs_h[(ib, 7, tsx)])
                            pd = ps_d[j % 2]
                            for ic in range(8):
                                ins = t.matmul(
                                    pd[:], wd_ib[:, ic, oc * 128:(oc + 1) * 128],
                                    h_ib[:, ic, tsx * 512:(tsx + 1) * 512],
                                    start=(ic == 0), stop=(ic == 7))
                            ctx.inc(ins, "TS")

            def p5_scalar(sc):
                for ib in range(8):
                    for ic in range(8):
                        for tsx in range(2):
                            gi = (ib * 8 + ic) * 2 + tsx
                            ctx.wait(sc, "TS", ts_gt[(ib, ic, tsx)])
                            if gi >= 2:
                                pk = ts_keys_g[gi - 2]
                                ctx.wait(sc, "VS", vs_h[pk])
                            ctx.inc(sc.activation(tg[gi % 2][:], ps_g[gi % 2][:],
                                                  AF.Silu), "SS")

            def p5_vector(v):
                for ib in range(8):
                    for ic in range(8):
                        for tsx in range(2):
                            gi = (ib * 8 + ic) * 2 + tsx
                            ctx.wait(v, "TS", ts_up[(ib, ic, tsx)])
                            ctx.wait(v, "SS", ss_si[(ib, ic, tsx)])
                            ctx.inc(v.tensor_mul(h_ib[:, ic, tsx * 512:(tsx + 1) * 512],
                                                 tg[gi % 2][:], ps_u[gi % 2][:]), "VS")
                    for tsx in range(2):
                        for oc in range(16):
                            j = (ib * 2 + tsx) * 16 + oc
                            ctx.wait(v, "TS", ts_dn[(ib, tsx, oc)])
                            dst = g_sb[:, oc, tsx * 512:(tsx + 1) * 512]
                            if ib == 0:
                                ins = v.tensor_copy(dst, ps_d[j % 2][:])
                            else:
                                ins = v.tensor_add(dst, dst, ps_d[j % 2][:])
                            ctx.inc(ins, "VS")

            with ExitStack() as sP5:
                ps_g = [sP5.enter_context(nc.psum_tensor(f"ps_g{i}", [128, 512], F32))
                        for i in range(2)]
                ps_u = [sP5.enter_context(nc.psum_tensor(f"ps_u{i}", [128, 512], F32))
                        for i in range(2)]
                ps_d = [sP5.enter_context(nc.psum_tensor(f"ps_d{i}", [128, 512], F32))
                        for i in range(2)]
                block.tensor(p5_tensor)
                block.scalar(p5_scalar)
                block.vector(p5_vector)
                assert ctx.cnt["TS"] == cTS and ctx.cnt["SS"] == cSS
                assert ctx.cnt["VS"] == cVS
            barrier(ctx, block)

        if _DEBUG:
            def dmp5(s):
                DBG.dma(s, dbg["g"], g_sb[:].rearrange("p a b -> p (a b)"))
                DBG.wait_all(s)
            block.sync(dmp5)
            for emitf in (block.tensor, block.vector, block.scalar, block.gpsimd):
                def fdbg5(eng):
                    DBG.wait_all(eng)
                emitf(fdbg5)
            barrier(ctx, block)
        if _PHASES <= 5:
            return nc

        with ExitStack() as sD:
            ed = sD.enter_context
            o_sb = ed(nc.sbuf_tensor("o_sb", [128, 16, 1024], BF16))
            wobuf = [ed(nc.sbuf_tensor(f"wobuf{i}", [128, 2048], BF16)) for i in range(2)]
            gsil = [ed(nc.sbuf_tensor(f"gsil{i}", [128, 512], F32)) for i in range(2)]
            u2_buf = [ed(nc.sbuf_tensor(f"u2_buf{i}", [128, D], F32)) for i in range(2)]
            out_buf = [ed(nc.sbuf_tensor(f"out_buf{i}", [128, D], F32)) for i in range(2)]
            WO = Slots(nc, glob, "wo", 2)
            OSB = Slots(nc, glob, "osb", 1)
            U2 = Slots(nc, glob, "u2", 2)
            OUT = Slots(nc, glob, "oo", 2)

            base = dict(ctx.cnt)
            cTS, cSS, cVS = base["TS"], base["SS"], base["VS"]
            wotok = [None] * 16
            ts_s = {}
            ss_gs = {}
            vs_sg = {}
            woc = [0, 0]
            for oc in range(16):
                woc[oc % 2] += 16; wotok[oc] = (oc % 2, woc[oc % 2])
                for tsx in range(2):
                    cTS += 1; ts_s[(oc, tsx)] = cTS
                    cSS += 1; ss_gs[(oc, tsx)] = cSS
                    cVS += 1; vs_sg[(oc, tsx)] = cVS
            vs_sg_all = cVS
            u2tok = [None] * NT_OWN
            outtok = [None] * NT_OWN
            ts_tp6 = [[0] * 16 for _ in range(NT_OWN)]
            vs_add = [[0] * 16 for _ in range(NT_OWN)]
            u2c = [0, 0]
            outc = [0, 0]
            for tt in range(NT_OWN):
                u2c[tt % 2] += 16; u2tok[tt] = (tt % 2, u2c[tt % 2])
                for oc in range(16):
                    cTS += 1; ts_tp6[tt][oc] = cTS
                    cVS += 1; vs_add[tt][oc] = cVS
                outc[tt % 2] += 16; outtok[tt] = (tt % 2, outc[tt % 2])

            def p6_sync(s):
                OS.wait_all(s)
                OSB.dma(s, o_sb[:], d_oscr.rearrange("a p b -> p a b"))
                for tt in range(NT_OWN):
                    if tt >= 2:
                        ctx.wait(s, "VS", vs_add[tt - 2][15])
                    tok = U2.dma(s, u2_buf[tt % 2][:],
                                 d_u[HALO + tt * 128:HALO + (tt + 1) * 128, :], tt % 2)
                    assert tok == u2tok[tt]
                    if tt >= 1:
                        ctx.wait(s, "VS", vs_add[tt - 1][15])
                        tok = OUT.dma(s, d_out[(tt - 1) * 128:tt * 128, :],
                                      out_buf[(tt - 1) % 2][:], (tt - 1) % 2)
                        assert tok == outtok[tt - 1]
                ctx.wait(s, "VS", vs_add[NT_OWN - 1][15])
                tok = OUT.dma(s, d_out[(NT_OWN - 1) * 128:NT_OWN * 128, :],
                              out_buf[(NT_OWN - 1) % 2][:], (NT_OWN - 1) % 2)
                assert tok == outtok[NT_OWN - 1]
                OUT.wait_all(s)

            def p6_gp(g):
                for oc in range(16):
                    if oc >= 2:
                        ctx.wait(g, "TS", ts_s[(oc - 2, 1)])
                    tok = WO.dma(g, wobuf[oc % 2][:], d_wo[oc], oc % 2)
                    assert tok == wotok[oc]
            block.gpsimd(p6_gp)

            def p6_tensor(t):
                OSB.wait_all(t)
                for oc in range(16):
                    WO.wait(t, wotok[oc])
                    for tsx in range(2):
                        j = oc * 2 + tsx
                        if j >= 2:
                            ctx.wait(t, "VS", vs_sg[(oc - 1, tsx)])
                        p = ps_s6[j % 2]
                        for kc in range(16):
                            ins = t.matmul(p[:], wobuf[oc % 2][:, kc * 128:(kc + 1) * 128],
                                           o_sb[:, kc, tsx * 512:(tsx + 1) * 512],
                                           start=(kc == 0), stop=(kc == 15))
                        ctx.inc(ins, "TS")
                for tt in range(NT_OWN):
                    for oc in range(16):
                        j6 = tt * 16 + oc
                        if j6 == 0:
                            ctx.wait(t, "VS", vs_sg_all)
                        if j6 >= 2:
                            pj = j6 - 2
                            ctx.wait(t, "VS", vs_add[pj // 16][pj % 16])
                        ctx.inc(t.transpose(ps_tr6[j6 % 2][:],
                                            g_sb[:, oc, tt * 128:(tt + 1) * 128],
                                            id_f[:]), "TS")

            def p6_scalar(sc):
                for oc in range(16):
                    for tsx in range(2):
                        j = oc * 2 + tsx
                        if j >= 2:
                            ctx.wait(sc, "VS", vs_sg[(oc - 1, tsx)])
                        ctx.inc(sc.activation(gsil[j % 2][:],
                                              g_sb[:, oc, tsx * 512:(tsx + 1) * 512],
                                              AF.Silu), "SS")

            def p6_vector(v):
                if _DEBUG:
                    DBG.wait_all(v)
                for oc in range(16):
                    for tsx in range(2):
                        j = oc * 2 + tsx
                        ctx.wait(v, "TS", ts_s[(oc, tsx)])
                        ctx.wait(v, "SS", ss_gs[(oc, tsx)])
                        ctx.inc(v.scalar_tensor_tensor(
                            g_sb[:, oc, tsx * 512:(tsx + 1) * 512],
                            ps_s6[j % 2][:], bo_pc[:, oc:oc + 1], gsil[j % 2][:],
                            OP.add, OP.mult), "VS")
                for tt in range(NT_OWN):
                    U2.wait(v, u2tok[tt])
                    for oc in range(16):
                        ctx.wait(v, "TS", ts_tp6[tt][oc])
                        if tt >= 2 and oc == 0:
                            OUT.wait(v, outtok[tt - 2])
                        ctx.inc(v.tensor_add(out_buf[tt % 2][:, oc * 128:(oc + 1) * 128],
                                             ps_tr6[(tt * 16 + oc) % 2][:],
                                             u2_buf[tt % 2][:, oc * 128:(oc + 1) * 128]),
                                "VS")

            with ExitStack() as sP6:
                ps_s6 = [sP6.enter_context(nc.psum_tensor(f"ps_s6{i}", [128, 512], F32))
                         for i in range(2)]
                ps_tr6 = [sP6.enter_context(nc.psum_tensor(f"ps_tr6{i}", [128, 128], F32))
                          for i in range(2)]
                block.sync(p6_sync)
                block.tensor(p6_tensor)
                block.scalar(p6_scalar)
                block.vector(p6_vector)
                assert ctx.cnt["TS"] == cTS and ctx.cnt["SS"] == cSS
                assert ctx.cnt["VS"] == cVS

            if _DEBUG:
                def dmp6(s):
                    DBG.dma(s, dbg["of"], d_oscr)
                    DBG.wait_all(s)
                block.sync(dmp6)

    return nc

def _prep_inputs(L0_attn_u, L0_attn_Wqkv, L0_attn_bqkv, L0_attn_Wo, L0_attn_bo,
                 L0_attn_sinks, L0_ffn_w_gate_up, L0_ffn_w_down):
    u = np.ascontiguousarray(L0_attn_u, dtype=np.float32)
    Wqkv = np.asarray(L0_attn_Wqkv, dtype=np.float32)
    bqkv = np.asarray(L0_attn_bqkv, dtype=np.float32)
    Wo = np.asarray(L0_attn_Wo, dtype=np.float32)
    bo = np.asarray(L0_attn_bo, dtype=np.float32)
    sinks = np.asarray(L0_attn_sinks, dtype=np.float32)
    Wgu = np.asarray(L0_ffn_w_gate_up, dtype=np.float32)
    Wd = np.asarray(L0_ffn_w_down, dtype=np.float32)

    A = Wqkv[:4096].reshape(32, 128, 16, 128)
    wqkv_pack = np.ascontiguousarray(A.transpose(0, 3, 2, 1)).reshape(32, 128, 2048)
    Bv = Wqkv[4096:].reshape(8, 256, 16, 128)
    v_pack = np.ascontiguousarray(Bv.transpose(0, 3, 2, 1)).reshape(8, 128, 4096)
    Ao = Wo.reshape(16, 128, 16, 128)
    wo_pack = np.ascontiguousarray(Ao.transpose(0, 3, 2, 1)).reshape(16, 128, 2048)
    G = Wgu.reshape(128, 128, 16, 128)
    pack_all = np.ascontiguousarray(G.transpose(0, 3, 2, 1)).reshape(128, 128, 2048)
    order = []
    for ib in range(8):
        for ic in range(8):
            order.append(ib * 8 + ic)
            order.append(64 + ib * 8 + ic)
    wgu_pack = np.ascontiguousarray(pack_all[order])
    wd_pack = np.ascontiguousarray(Wd.T.reshape(64, 128, 2048))

    bqkv_pc = np.ascontiguousarray(bqkv[:4096].reshape(32, 128).T)
    swp = np.concatenate([np.arange(32, 64), np.arange(0, 32),
                          np.arange(96, 128), np.arange(64, 96)])
    bqkv_sw = np.ascontiguousarray(bqkv_pc[swp, :])
    bqkv_pc48 = np.ascontiguousarray(bqkv.reshape(48, 128).T)
    vbias_row = np.ascontiguousarray(bqkv[None, 4096:])
    bo_pc = np.ascontiguousarray(bo.reshape(16, 128).T)
    sinks_t = np.ascontiguousarray(sinks.transpose(0, 2, 1))
    sinks_n = np.ascontiguousarray(sinks)
    ident = np.eye(128, dtype=np.float32)
    ones_row = np.ones((1, EXT), dtype=np.float32)
    consts = np.broadcast_to(np.array([1e-5, -1.0, 1.0 / 2048.0, 0.125], np.float32),
                             (128, 4)).copy()

    inv = (10000.0 ** (-np.arange(32, dtype=np.float64) / 32)).astype(np.float64)

    def rope_tables(pos, scale):
        ang = np.outer(inv, pos.astype(np.float64))
        C = np.tile(np.cos(ang), (4, 1)) * scale
        sn = np.sin(ang)
        S = np.concatenate([-sn, sn, -sn, sn], axis=0) * scale
        return (np.ascontiguousarray(C, dtype=np.float32),
                np.ascontiguousarray(S, dtype=np.float32))

    tabs = []
    masks_q = []
    nb = L // 64
    for q4 in range(4):
        pos_k = np.clip(q4 * TOK - HALO + np.arange(EXT), 0, L - 1)
        ck, sk = rope_tables(pos_k, 1.0)
        m = np.zeros((NT_OWN, 128, 272), np.float32)
        ii = np.arange(128)[:, None]
        jj = np.arange(256)[None, :]
        band = np.where((jj >= ii) & (jj <= ii + 128), 0.0, NEG).astype(np.float32)
        for qt in range(NT_OWN):
            mm = band.copy()
            n0 = q4 * 16 + qt * 2
            if n0 - 1 < 0:
                mm[:, 0:64] = NEG
            if n0 + 2 > nb - 1:
                mm[:, 192:256] = NEG
            m[qt, :, NS:272] = mm
        tabs.append((ck, sk))
        masks_q.append(m)

    in_maps = []
    for c in range(NCORE):
        b = c // 4
        q4 = c % 4
        start = q4 * TOK
        u_ext = np.zeros((EXT, D), np.float32)
        lo = max(0, start - HALO)
        hi = min(L, start + TOK + HALO)
        u_ext[lo - (start - HALO):hi - (start - HALO)] = u[b, lo:hi]
        ck, sk = tabs[q4]
        in_maps.append(dict(
            u_ext=u_ext, wqkv_pack=wqkv_pack, v_pack=v_pack, wo_pack=wo_pack,
            wgu_pack=wgu_pack, wd_pack=wd_pack, bqkv_pc=bqkv_pc48, bqkv_sw=bqkv_sw, bo_pc=bo_pc,
            sinks_t=sinks_t, sinks_n=sinks_n, ck=ck, sk=sk, vbias_row=vbias_row,
            masks=masks_q[q4], ident=ident, ones_row=ones_row, consts=consts,
        ))
    return in_maps


def kernel(L0_attn_u, L0_attn_Wqkv, L0_attn_bqkv, L0_attn_Wo, L0_attn_bo,
           L0_attn_sinks, L0_ffn_w_gate_up, L0_ffn_w_down):
    in_maps = _prep_inputs(L0_attn_u, L0_attn_Wqkv, L0_attn_bqkv, L0_attn_Wo,
                           L0_attn_bo, L0_attn_sinks, L0_ffn_w_gate_up,
                           L0_ffn_w_down)
    nc = build_program2()
    res = run_bass_kernel_spmd(nc, in_maps, core_ids=list(range(NCORE)))
    out = np.empty((B, L, D), np.float32)
    for c in range(NCORE):
        b = c // 4
        q4 = c % 4
        out[b, q4 * TOK:(q4 + 1) * TOK] = res.results[c]["out"]
    return out


# version 07
# speedup vs baseline: 1.0842x; 1.0001x over previous
import numpy as np
from contextlib import ExitStack

import concourse.bass as bass
import concourse.mybir as mybir
from concourse.bass_utils import run_bass_kernel_spmd

F32 = mybir.dt.float32
BF16 = mybir.dt.bfloat16
AX = mybir.AxisListType.X
AF = mybir.ActivationFunctionType
OP = mybir.AluOpType

B, L, D = 2, 4096, 2048
NCORE = 8
TOK = 1024
HALO = 64
EXT = TOK + 2 * HALO
NT_EXT = EXT // 128
NT_OWN = TOK // 128
DH = 64
NH = 32
NS = 16
NEG = -1.0e30

_DEBUG = False
_PHASES = 6


class Slots:

    def __init__(self, nc, es, name, n):
        self.sems = [es.enter_context(nc.semaphore(f"{name}{i}")) for i in range(n)]
        self.cnt = [0] * n

    def dma(self, q, out, in_, slot=0):
        ins = q.dma_start(out=out, in_=in_)
        ins.then_inc(self.sems[slot], 16)
        self.cnt[slot] += 16
        return (slot, self.cnt[slot])

    def wait(self, eng, tok):
        slot, val = tok
        eng.wait_ge(self.sems[slot], val)

    def wait_all(self, eng):
        for i, c in enumerate(self.cnt):
            if c > 0:
                eng.wait_ge(self.sems[i], c)


class Ctx:
    def __init__(self, nc, es):
        self.nc = nc
        self.cnt = {}
        self.sem = {}
        for name in ["TS", "VS", "SS"]:
            self.sem[name] = es.enter_context(nc.semaphore(name))
            self.cnt[name] = 0

    def inc(self, instr, name, n=1):
        instr.then_inc(self.sem[name], n)
        self.cnt[name] += n
        return self.cnt[name]

    def wait(self, eng, name, val):
        if val > 0:
            eng.wait_ge(self.sem[name], val)


def barrier(ctx, block):
    snap = dict(ctx.cnt)
    owners = {"TS": "tensor", "VS": "vector", "SS": "scalar"}

    def mk(engname):
        def f(eng):
            for s, v in snap.items():
                if v > 0 and owners[s] != engname:
                    eng.wait_ge(ctx.sem[s], v)
        return f
    block.tensor(mk("tensor"))
    block.vector(mk("vector"))
    block.scalar(mk("scalar"))
    block.gpsimd(mk("gpsimd"))
    block.sync(mk("sync"))


def build_program2():
    nc = bass.Bass("TRN2", target_bir_lowering=False)

    d_u = nc.dram_tensor("u_ext", [EXT, D], F32, kind="ExternalInput").ap()
    d_wqkv = nc.dram_tensor("wqkv_pack", [32, 128, 2048], F32, kind="ExternalInput").ap()
    d_vpk = nc.dram_tensor("v_pack", [8, 128, 4096], F32, kind="ExternalInput").ap()
    d_wo = nc.dram_tensor("wo_pack", [16, 128, 2048], F32, kind="ExternalInput").ap()
    d_wgu = nc.dram_tensor("wgu_pack", [128, 128, 2048], F32, kind="ExternalInput").ap()
    d_wd = nc.dram_tensor("wd_pack", [64, 128, 2048], F32, kind="ExternalInput").ap()
    d_bqkv_pc = nc.dram_tensor("bqkv_pc", [128, 48], F32, kind="ExternalInput").ap()
    d_bqkv_sw = nc.dram_tensor("bqkv_sw", [128, 32], F32, kind="ExternalInput").ap()
    d_vbias = nc.dram_tensor("vbias_row", [1, 2048], F32, kind="ExternalInput").ap()
    d_bo = nc.dram_tensor("bo_pc", [128, 16], F32, kind="ExternalInput").ap()
    d_sinkt = nc.dram_tensor("sinks_t", [NH, DH, NS], F32, kind="ExternalInput").ap()
    d_sinkn = nc.dram_tensor("sinks_n", [NH, NS, DH], F32, kind="ExternalInput").ap()
    d_ck = nc.dram_tensor("ck", [128, EXT], F32, kind="ExternalInput").ap()
    d_sk = nc.dram_tensor("sk", [128, EXT], F32, kind="ExternalInput").ap()
    d_masks = nc.dram_tensor("masks", [NT_OWN, 128, 272], F32, kind="ExternalInput").ap()
    d_ident = nc.dram_tensor("ident", [128, 128], F32, kind="ExternalInput").ap()
    d_ones = nc.dram_tensor("ones_row", [1, EXT], F32, kind="ExternalInput").ap()
    d_consts = nc.dram_tensor("consts", [128, 4], F32, kind="ExternalInput").ap()
    d_out = nc.dram_tensor("out", [TOK, D], F32, kind="ExternalOutput").ap()
    d_oscr = nc.dram_tensor("o_scr", [16, 128, 1024], BF16, kind="Internal").ap()

    dbg = {}
    if _DEBUG:
        dbg["xnf"] = nc.dram_tensor("d_xnf", [128, 16 * EXT], BF16, kind="ExternalOutput").ap()
        dbg["qf"] = nc.dram_tensor("d_qf", [128, 16 * TOK], BF16, kind="ExternalOutput").ap()
        dbg["kf"] = nc.dram_tensor("d_kf", [128, 16 * EXT], BF16, kind="ExternalOutput").ap()
        dbg["vt"] = nc.dram_tensor("d_vt", [128, NT_EXT * 2048], BF16, kind="ExternalOutput").ap()
        dbg["of"] = nc.dram_tensor("d_of", [16, 128, 1024], BF16, kind="ExternalOutput").ap()
        dbg["g"] = nc.dram_tensor("d_g", [128, 16 * TOK], F32, kind="ExternalOutput").ap()

    with ExitStack() as glob:
        eg = glob.enter_context
        ctx = Ctx(nc, glob)
        DBG = Slots(nc, glob, "dbg", 1) if _DEBUG else None

        xn_f = eg(nc.sbuf_tensor("xn_f", [128, 16, EXT], BF16))
        consts = eg(nc.sbuf_tensor("consts_sb", [128, 4], F32))
        bo_pc = eg(nc.sbuf_tensor("bo_sb", [128, 16], F32))
        id_f = eg(nc.sbuf_tensor("id_f", [128, 128], F32))
        id_bf = eg(nc.sbuf_tensor("id_bf", [128, 128], BF16))
        sumv9 = eg(nc.sbuf_tensor("sumv9", [128, NT_EXT], F32))
        sumsq9 = eg(nc.sbuf_tensor("sumsq9", [128, NT_EXT], F32))
        mean9 = eg(nc.sbuf_tensor("mean9", [128, NT_EXT], F32))
        sqv9 = eg(nc.sbuf_tensor("sqv9", [128, NT_EXT], F32))
        rstd9 = eg(nc.sbuf_tensor("rstd9", [128, NT_EXT], F32))
        tmp9 = eg(nc.sbuf_tensor("tmp9", [128, NT_EXT], F32))
        negmr9 = eg(nc.sbuf_tensor("negmr9", [128, NT_EXT], F32))

        block = eg(nc.Block())

        CS = Slots(nc, glob, "cs", 1)
        CG = Slots(nc, glob, "cg", 1)

        def p0_sync(s):
            CS.dma(s, id_f[:], d_ident)
            CS.dma(s, consts[:], d_consts)
            CS.dma(s, bo_pc[:], d_bo)

        def p0_gp(g):
            CG.dma(g, id_bf[:], d_ident)
        block.sync(p0_sync)
        block.gpsimd(p0_gp)
        for emit in (block.tensor, block.vector, block.scalar):
            def f(eng):
                CS.wait_all(eng)
                CG.wait_all(eng)
            emit(f)

        sP1mem = glob.enter_context(ExitStack())
        u_all = sP1mem.enter_context(nc.sbuf_tensor("u_all", [128, NT_EXT, D], F32))
        xn_bf = [sP1mem.enter_context(nc.sbuf_tensor(f"xn_bf{i}", [128, D], BF16))
                 for i in range(2)]
        sq_scr = sP1mem.enter_context(nc.sbuf_tensor("sq_scr", [128, D], BF16))
        U = Slots(nc, glob, "u", NT_EXT)

        base = dict(ctx.cnt)
        utok = [None] * NT_EXT
        ss_sq = [0] * NT_EXT
        ss_xn = [0] * NT_EXT
        vs_batA = 0
        ss_sqrt9 = 0
        vs_batB = 0
        ts_tr = [[0] * 16 for _ in range(NT_EXT)]
        vs_ev = [[0] * 16 for _ in range(NT_EXT)]
        cSS, cVS, cTS = base["SS"], base["VS"], base["TS"]
        for tt in range(NT_EXT):
            utok[tt] = (tt, 16)
            cSS += 1; ss_sq[tt] = cSS
        cVS += 1; vs_batA = cVS
        cSS += 1; ss_sqrt9 = cSS
        cVS += 1; vs_batB = cVS
        for tt in range(NT_EXT):
            cSS += 1; ss_xn[tt] = cSS
            for kc in range(16):
                cTS += 1; ts_tr[tt][kc] = cTS
                cVS += 1; vs_ev[tt][kc] = cVS

        def p1_sync(s):
            for tt in range(NT_EXT):
                tok = U.dma(s, u_all[:, tt, :], d_u[tt * 128:(tt + 1) * 128, :], tt)
                assert tok == utok[tt]
        block.sync(p1_sync)

        def p1_scalar(sc):
            for tt in range(NT_EXT):
                U.wait(sc, utok[tt])
                ctx.inc(sc.activation(sq_scr[:], u_all[:, tt, :], AF.Square,
                                      accum_out=sumsq9[:, tt:tt + 1]), "SS")
                sc.drain()
            ctx.wait(sc, "VS", vs_batA)
            ctx.inc(sc.activation(sqv9[:], tmp9[:], AF.Sqrt), "SS")
            ctx.wait(sc, "VS", vs_batB)
            for tt in range(NT_EXT):
                if tt >= 2:
                    ctx.wait(sc, "TS", ts_tr[tt - 2][15])
                ctx.inc(sc.activation(xn_bf[tt % 2][:], u_all[:, tt, :], AF.Identity,
                                      bias=negmr9[:, tt:tt + 1],
                                      scale=rstd9[:, tt:tt + 1]), "SS")
        block.scalar(p1_scalar)

        def p1_tensor(t):
            for tt in range(NT_EXT):
                ctx.wait(t, "SS", ss_xn[tt])
                for kc in range(16):
                    if tt * 16 + kc >= 2:
                        prev = tt * 16 + kc - 2
                        ctx.wait(t, "VS", vs_ev[prev // 16][prev % 16])
                    ctx.inc(t.transpose(ps_tr[kc % 2][:],
                                        xn_bf[tt % 2][:, kc * 128:(kc + 1) * 128],
                                        id_bf[:]), "TS")

        def p1_vector(v):
            for tt in range(NT_EXT):
                U.wait(v, utok[tt])
                v.reduce_sum(sumv9[:, tt:tt + 1], u_all[:, tt, :], axis=AX)
            v.drain()
            v.tensor_scalar_mul(mean9[:], sumv9[:], consts[:, 2:3])
            v.drain()
            v.tensor_mul(sumv9[:], mean9[:], mean9[:])
            ctx.wait(v, "SS", ss_sq[NT_EXT - 1])
            v.tensor_scalar_mul(sumsq9[:], sumsq9[:], consts[:, 2:3])
            v.drain()
            v.tensor_sub(tmp9[:], sumsq9[:], sumv9[:])
            v.drain()
            ctx.inc(v.tensor_scalar(tmp9[:], tmp9[:], consts[:, 0:1], None,
                                    OP.add), "VS")
            ctx.wait(v, "SS", ss_sqrt9)
            v.reciprocal(rstd9[:], sqv9[:])
            v.drain()
            v.tensor_mul(tmp9[:], mean9[:], rstd9[:])
            v.drain()
            ctx.inc(v.tensor_scalar_mul(negmr9[:], tmp9[:], consts[:, 1:2]), "VS")
            for tt in range(NT_EXT):
                for kc in range(16):
                    ctx.wait(v, "TS", ts_tr[tt][kc])
                    ctx.inc(v.tensor_copy(xn_f[:, kc, tt * 128:(tt + 1) * 128],
                                          ps_tr[kc % 2][:]), "VS")
        with ExitStack() as sP1:
            ps_tr = [sP1.enter_context(nc.psum_tensor(f"ps_tr{i}", [128, 128], BF16))
                     for i in range(2)]
            block.tensor(p1_tensor)
            block.vector(p1_vector)
        assert ctx.cnt["SS"] == cSS and ctx.cnt["VS"] == cVS and ctx.cnt["TS"] == cTS
        barrier(ctx, block)

        if _DEBUG:
            def dmp(s):
                DBG.dma(s, dbg["xnf"], xn_f[:].rearrange("p a b -> p (a b)"))
                DBG.wait_all(s)
            block.sync(dmp)
            for emitf in (block.tensor, block.vector, block.scalar, block.gpsimd):
                def fdbg1(eng):
                    DBG.wait_all(eng)
                emitf(fdbg1)
            barrier(ctx, block)
        if _PHASES <= 1:
            return nc

        sP1mem.close()
        OS_outer = [None]
        with ExitStack() as sA:
            ea = sA.enter_context
            q_f = ea(nc.sbuf_tensor("q_f", [128, 16, TOK], BF16))
            k_f = ea(nc.sbuf_tensor("k_f", [128, 16, EXT], BF16))
            v_t = ea(nc.sbuf_tensor("v_t", [128, NT_EXT, 2048], BF16))
            ck_t = ea(nc.sbuf_tensor("ck_t", [128, EXT], F32))
            sk_t = ea(nc.sbuf_tensor("sk_t", [128, EXT], F32))
            wx = [ea(nc.sbuf_tensor(f"wx{i}", [128, 4096], BF16)) for i in range(3)]
            t1 = ea(nc.sbuf_tensor("t1", [128, 512], F32))
            m2 = ea(nc.sbuf_tensor("m2", [128, 512], F32))
            t2sw = [ea(nc.sbuf_tensor(f"t2sw{i}", [128, 512], F32)) for i in range(2)]
            ones_bf = ea(nc.sbuf_tensor("ones_bf", [1, EXT], BF16))
            bq_pc = ea(nc.sbuf_tensor("bq_pc", [128, 48], F32))
            bq_sw = ea(nc.sbuf_tensor("bq_sw", [128, 32], F32))
            vbias_bf = ea(nc.sbuf_tensor("vbias_bf", [1, 2048], BF16))
            sinkt_bf = ea(nc.sbuf_tensor("sinkt_bf", [128, NH * NS], BF16))
            sinkn_bf = ea(nc.sbuf_tensor("sinkn_bf", [NS, NH * DH], BF16))
            masks_bf = ea(nc.sbuf_tensor("masks_bf", [128, NT_OWN, 272], BF16))
            o_buf = [ea(nc.sbuf_tensor(f"o_buf{i}", [128, 1024], BF16)) for i in range(2)]
            pA_bf = [ea(nc.sbuf_tensor(f"pA_bf{i}", [128, 272], BF16)) for i in range(2)]
            pB_bf = [ea(nc.sbuf_tensor(f"pB_bf{i}", [128, 272], BF16)) for i in range(2)]
            pt_sb = [ea(nc.sbuf_tensor(f"pt_sb{i}", [128, 512], BF16)) for i in range(2)]
            pts_sb = [ea(nc.sbuf_tensor(f"pts_sb{i}", [NS, 256], BF16)) for i in range(2)]
            nmA = [ea(nc.sbuf_tensor(f"nmA{i}", [128, 1], F32)) for i in range(2)]
            nmB = [ea(nc.sbuf_tensor(f"nmB{i}", [128, 1], F32)) for i in range(2)]
            n2A = [ea(nc.sbuf_tensor(f"n2A{i}", [128, 1], F32)) for i in range(2)]
            n2B = [ea(nc.sbuf_tensor(f"n2B{i}", [128, 1], F32)) for i in range(2)]
            rsA = [ea(nc.sbuf_tensor(f"rsA{i}", [128, 1], F32)) for i in range(2)]
            rsB = [ea(nc.sbuf_tensor(f"rsB{i}", [128, 1], F32)) for i in range(2)]
            rcA = [ea(nc.sbuf_tensor(f"rcA{i}", [128, 1], F32)) for i in range(2)]
            rcB = [ea(nc.sbuf_tensor(f"rcB{i}", [128, 1], F32)) for i in range(2)]

            TB = Slots(nc, glob, "tb", 1)
            CB = Slots(nc, glob, "cb", 1)
            MS = Slots(nc, glob, "ms", 1)
            WX = Slots(nc, glob, "wxs", 3)
            OS = Slots(nc, glob, "os", 2)
            OS_outer[0] = OS

            def p2_consts(g):
                CB.dma(g, ones_bf[:], d_ones)
                CB.dma(g, vbias_bf[:], d_vbias)
                MS.dma(g, sinkt_bf[0:64, :].rearrange("d (h s) -> d h s", h=NH),
                       d_sinkt.rearrange("h d s -> d h s"))
                MS.dma(g, sinkt_bf[64:128, :].rearrange("d (h s) -> d h s", h=NH),
                       d_sinkt.rearrange("h d s -> d h s"))
                MS.dma(g, sinkn_bf[:].rearrange("s (h d) -> s h d", h=NH),
                       d_sinkn.rearrange("h s d -> s h d"))
                MS.dma(g, masks_bf[:], d_masks.rearrange("q p c -> p q c"))
            block.gpsimd(p2_consts)

            def p2_tables(s):
                TB.dma(s, ck_t[:], d_ck)
                TB.dma(s, sk_t[:], d_sk)
                TB.dma(s, bq_pc[:], d_bqkv_pc)
                TB.dma(s, bq_sw[:], d_bqkv_sw)
            block.sync(p2_tables)

            q_ts = [(0, 512), (512, 512)]
            k_ts = [(0, 512), (512, 512), (1024, 128)]

            loads = []
            units = []
            unit_of_step = [[] for _ in range(16)]
            for c in range(16):
                loads.append(("q", c))
                li_q = len(loads) - 1
                for tsi, (off, ln) in enumerate(q_ts):
                    unit_of_step[c].append(("qk", c, tsi, off, ln, li_q))
                loads.append(("k", 16 + c))
                li_k = len(loads) - 1
                for tsi, (off, ln) in enumerate(k_ts):
                    unit_of_step[c].append(("qk", 16 + c, tsi, off, ln, li_k))
                if c % 2 == 0:
                    loads.append(("v", c // 2))
                    li_v = len(loads) - 1
                    for tt in range(NT_EXT):
                        unit_of_step[c].append(("v", c // 2, tt, li_v))
            for c in range(16):
                units.extend(unit_of_step[c])
            NU = len(units)
            unit_idx = {u: i for i, u in enumerate(units)}
            last_unit_of_load = {}
            for i, u in enumerate(units):
                last_unit_of_load[u[-1]] = i

            pairs = [(c, qt) for c in range(16) for qt in range(NT_OWN)]
            NP3 = len(pairs)
            rope_done_of_chunk = [0] * 16

            base = dict(ctx.cnt)
            cTS, cVS, cSS = base["TS"], base["VS"], base["SS"]
            wxtok = [None] * len(loads)
            ts_mm = [0] * NU
            vs_t1 = [0] * NU
            ss_sw = [0] * NU
            vs_rope = [0] * NU
            ts_sc = [0] * NP3
            ts_tp = [0] * NP3
            ts_av = [0] * NP3
            vs_mx = [0] * NP3
            vs_nm = [0] * NP3
            vs_pt = [0] * NP3
            ss_ex = [0] * NP3
            ss_oe = [0] * NP3
            ostok = [None] * 16
            oc_ = [0, 0]
            wxc = [0, 0, 0]
            for li in range(len(loads)):
                sl = li % 3
                wxc[sl] += 16
                wxtok[li] = (sl, wxc[sl])
            for c in range(16):
                oc_[c % 2] += 16
                ostok[c] = (c % 2, oc_[c % 2])

            def p3_window(c):
                if c == 0:
                    return []
                return list(range((c - 1) * NT_OWN, c * NT_OWN))

            def step_items(c):
                U = unit_of_step[c] if c < 16 else []
                P = p3_window(c)
                items = []
                for k in range(max(len(U), len(P))):
                    if k < len(U):
                        items.append(("u", unit_idx[U[k]]))
                    if k < len(P):
                        items.append(("p", P[k]))
                return items, P

            tsched = []
            vsched = []
            ssched = []
            for c in range(17):
                items, P = step_items(c)
                w0 = P[0] if P else 0
                for kind, x in items:
                    if kind == "u":
                        tsched.append(("u", x))
                        if units[x][0] == "qk":
                            vsched.append(("u", x))
                        ssched.append(("u", x))
                    else:
                        j = x
                        tsched.append(("sc", j))
                        if j - 1 >= w0:
                            tsched.append(("tp", j - 1))
                        if j - 2 >= w0:
                            tsched.append(("av", j - 2))
                        vsched.append(("mx", j))
                        vsched.append(("nm", j))
                        if j - 1 >= w0:
                            vsched.append(("ev", j - 1))
                        ssched.append(("ex", j))
                        if j - 2 >= w0:
                            ssched.append(("oe", j - 2))
                if P:
                    w1 = P[-1] + 1
                    tsched.append(("av", w1 - 2))
                    tsched.append(("tp", w1 - 1))
                    tsched.append(("av", w1 - 1))
                    vsched.append(("ev", w1 - 1))
                    ssched.append(("oe", w1 - 2))
                    ssched.append(("oe", w1 - 1))

            for kind, x in tsched:
                cTS += 1
                if kind == "u":
                    ts_mm[x] = cTS
                elif kind == "sc":
                    ts_sc[x] = cTS
                elif kind == "tp":
                    ts_tp[x] = cTS
                else:
                    ts_av[x] = cTS
            for kind, x in vsched:
                if kind == "u":
                    cVS += 1; vs_t1[x] = cVS
                    cVS += 1; vs_rope[x] = cVS
                    u = units[x]
                    if u[0] == "qk" and u[1] >= 16:
                        rope_done_of_chunk[u[1] - 16] = cVS
                elif kind == "mx":
                    cVS += 1; vs_mx[x] = cVS
                elif kind == "nm":
                    cVS += 1; vs_nm[x] = cVS
                else:
                    cVS += 1; vs_pt[x] = cVS
            for i_, u in enumerate(units):
                if u[0] == "v":
                    pass
            run = 0
            vs_after_unit = [0] * NU
            for i_, u in enumerate(units):
                if u[0] == "v":
                    vs_t1[i_] = 0
                    vs_rope[i_] = 0
            for kind, x in ssched:
                cSS += 1
                if kind == "u":
                    ss_sw[x] = cSS
                elif kind == "ex":
                    ss_ex[x] = cSS
                else:
                    ss_oe[x] = cSS

            def pm_gp(g):
                for li, ld in enumerate(loads):
                    if li >= 3:
                        ctx.wait(g, "TS", ts_mm[last_unit_of_load[li - 3]])
                    kind = ld[0]
                    if kind == "q":
                        src = d_wqkv[ld[1]]
                        tok = WX.dma(g, wx[li % 3][:, 0:2048], src, li % 3)
                    elif kind == "k":
                        src = d_wqkv[ld[1]]
                        tok = WX.dma(g, wx[li % 3][:, 0:2048], src, li % 3)
                    else:
                        src = d_vpk[ld[1]]
                        tok = WX.dma(g, wx[li % 3][:], src, li % 3)
                    assert tok == wxtok[li]
            block.gpsimd(pm_gp)

            def emit_unit_mm(t, u):
                i = unit_idx[u]
                if u[0] == "qk":
                    _, oc, tsi, off, ln, li = u
                    if tsi == 0:
                        WX.wait(t, wxtok[li])
                    if i >= 2:
                        ctx.wait(t, "VS", vs_rope[i - 2])
                        ctx.wait(t, "SS", ss_sw[i - 2])
                    p = ps_qk[i % 2]
                    if oc < 16:
                        src = xn_f[:, :, HALO + off:HALO + off + ln]
                    else:
                        src = xn_f[:, :, off:off + ln]
                    w = wx[li % 3]
                    for kc in range(16):
                        ins = t.matmul(p[:, 0:ln], w[:, kc * 128:(kc + 1) * 128],
                                       src[:, kc, :], start=(kc == 0), stop=(kc == 15))
                    ctx.inc(ins, "TS")
                else:
                    _, gp, tt, li = u
                    if tt == 0:
                        WX.wait(t, wxtok[li])
                    if i >= 2:
                        ctx.wait(t, "VS", vs_rope[i - 2])
                        ctx.wait(t, "SS", ss_sw[i - 2])
                    p = ps_qk[i % 2]
                    w = wx[li % 3]
                    for kc in range(16):
                        t.matmul(p[:, 0:256], xn_f[:, kc, tt * 128:(tt + 1) * 128],
                                 w[:, kc * 256:(kc + 1) * 256],
                                 start=(kc == 0), stop=False)
                    ctx.inc(t.matmul(p[:, 0:256], ones_bf[0:1, 0:128],
                                     vbias_bf[0:1, gp * 256:(gp + 1) * 256],
                                     start=False, stop=True), "TS")

            def p3_emit_scores(t, j):
                c, qt = pairs[j]
                if j >= 2:
                    ctx.wait(t, "SS", ss_ex[j - 2])
                if qt == 0:
                    ctx.wait(t, "VS", rope_done_of_chunk[c])
                for hh in range(2):
                    h = 2 * c + hh
                    po = hh * 64
                    qsl = q_f[po:po + 64, c, qt * 128:(qt + 1) * 128]
                    ksl = k_f[po:po + 64, c, qt * 128:qt * 128 + 256]
                    ps = (psA if hh == 0 else psB)[j % 2]
                    t.matmul(ps[:], id_bf[:], masks_bf[:, qt, :], start=True, stop=False)
                    t.matmul(ps[:, 0:NS], qsl, sinkt_bf[po:po + 64, h * NS:(h + 1) * NS],
                             start=False, stop=False)
                    ins = t.matmul(ps[:, NS:272], qsl, ksl, start=False, stop=True)
                ctx.inc(ins, "TS")

            def p3_emit_tp(t, i):
                ctx.wait(t, "VS", vs_nm[i])
                if i >= 1:
                    ctx.wait(t, "VS", vs_pt[i - 1])
                pt = ps_tp
                pa, pb = pA_bf[i % 2], pB_bf[i % 2]
                t.transpose(pt[:, 0:128], pa[:, NS:NS + 128], id_bf[:])
                t.transpose(pt[:, 128:256], pa[:, NS + 128:NS + 256], id_bf[:])
                t.transpose(pt[0:NS, 256:384], pa[:, 0:NS], id_bf[:])
                t.transpose(pt[:, 384:512], pb[:, NS:NS + 128], id_bf[:])
                t.transpose(pt[:, 512:640], pb[:, NS + 128:NS + 256], id_bf[:])
                ins = t.transpose(pt[0:NS, 640:768], pb[:, 0:NS], id_bf[:])
                ctx.inc(ins, "TS")

            def p3_emit_av(t, i):
                c, qt = pairs[i]
                ctx.wait(t, "VS", vs_pt[i])
                if i >= 1:
                    ctx.wait(t, "SS", ss_oe[i - 1])
                vu = [u for u in unit_of_step[2 * (c // 2)] if u[0] == "v"][-1]
                ctx.wait(t, "SS", ss_sw[unit_idx[vu]])
                for hh in range(2):
                    h = 2 * c + hh
                    co = hh * 256
                    tp_ = (0, 0) if hh == 0 else (0, 64)
                    outp = ps_o[0:64, :] if hh == 0 else ps_o[64:128, :]
                    t.matmul(outp, v_t[:, qt, h * 64:(h + 1) * 64],
                             pt_sb[i % 2][:, co:co + 128], start=True, stop=False,
                             tile_position=tp_)
                    t.matmul(outp, v_t[:, qt + 1, h * 64:(h + 1) * 64],
                             pt_sb[i % 2][:, co + 128:co + 256], start=False, stop=False,
                             tile_position=tp_)
                    ins = t.matmul(outp, sinkn_bf[:, h * 64:(h + 1) * 64],
                                   pts_sb[i % 2][:, hh * 128:hh * 128 + 128],
                                   start=False, stop=True, tile_position=tp_)
                ctx.inc(ins, "TS")

            def pm_tensor(t):
                CB.wait_all(t)
                MS.wait_all(t)
                for kind, x in tsched:
                    if kind == "u":
                        emit_unit_mm(t, units[x])
                    elif kind == "sc":
                        p3_emit_scores(t, x)
                    elif kind == "tp":
                        p3_emit_tp(t, x)
                    else:
                        p3_emit_av(t, x)

            def pm_vector(v):
                TB.wait_all(v)
                for kind, x in vsched:
                    if kind == "u":
                        u = units[x]
                        i = x
                        _, oc, tsi, off, ln, li = u
                        p = ps_qk[i % 2]
                        sw = t2sw[i % 2]
                        ctx.wait(v, "TS", ts_mm[i])
                        if oc < 16:
                            cs, ss_ = (ck_t[:, HALO + off:HALO + off + ln],
                                       sk_t[:, HALO + off:HALO + off + ln])
                            dst = q_f[:, oc, off:off + ln]
                        else:
                            cs, ss_ = (ck_t[:, off:off + ln],
                                       sk_t[:, off:off + ln])
                            dst = k_f[:, oc - 16, off:off + ln]
                        ctx.inc(v.scalar_tensor_tensor(
                            t1[:, 0:ln], p[:, 0:ln], bq_pc[:, oc:oc + 1], cs,
                            OP.add, OP.mult), "VS")
                        ctx.wait(v, "SS", ss_sw[i])
                        v.scalar_tensor_tensor(
                            m2[:, 0:ln], sw[:, 0:ln], bq_sw[:, oc:oc + 1], ss_,
                            OP.add, OP.mult)
                        v.drain()
                        ctx.inc(v.tensor_add(dst, t1[:, 0:ln], m2[:, 0:ln]), "VS")
                        v.drain()
                    elif kind == "mx":
                        j = x
                        ctx.wait(v, "TS", ts_sc[j])
                        v.reduce_max(nmA[j % 2][:], psA[j % 2][:], axis=AX, negate=True)
                        v.reduce_max(nmB[j % 2][:], psB[j % 2][:], axis=AX, negate=True)
                        v.drain()
                        v.tensor_mul(n2A[j % 2][:], nmA[j % 2][:], consts[:, 3:4])
                        ctx.inc(v.tensor_mul(n2B[j % 2][:], nmB[j % 2][:],
                                             consts[:, 3:4]), "VS")
                    elif kind == "nm":
                        j = x
                        ctx.wait(v, "SS", ss_ex[j])
                        v.reciprocal(rcA[j % 2][:], rsA[j % 2][:])
                        v.reciprocal(rcB[j % 2][:], rsB[j % 2][:])
                        v.drain()
                        v.tensor_scalar_mul(pA_bf[j % 2][:], pA_bf[j % 2][:],
                                            rcA[j % 2][:])
                        ctx.inc(v.tensor_scalar_mul(pB_bf[j % 2][:], pB_bf[j % 2][:],
                                                    rcB[j % 2][:]), "VS")
                    else:
                        j = x
                        ctx.wait(v, "TS", ts_tp[j])
                        v.tensor_copy(pt_sb[j % 2][:, 0:256], ps_tp[:, 0:256])
                        v.tensor_copy(pt_sb[j % 2][:, 256:512], ps_tp[:, 384:640])
                        v.tensor_copy(pts_sb[j % 2][:, 0:128], ps_tp[0:NS, 256:384])
                        ctx.inc(v.tensor_copy(pts_sb[j % 2][:, 128:256],
                                              ps_tp[0:NS, 640:768]), "VS")

            def pm_scalar(sc):
                def emit_oevac(i):
                    c, qt = pairs[i]
                    ctx.wait(sc, "TS", ts_av[i])
                    if qt == 0 and c >= 2:
                        OS.wait(sc, ostok[c - 2])
                    sc.activation(o_buf[c % 2][0:64, qt * 128:(qt + 1) * 128],
                                  ps_o[0:64, :], AF.Copy)
                    ctx.inc(sc.activation(o_buf[c % 2][64:128,
                                                     qt * 128:(qt + 1) * 128],
                                          ps_o[64:128, :], AF.Copy), "SS")
                for kind, x in ssched:
                    if kind == "u":
                        u = units[x]
                        i = x
                        p = ps_qk[i % 2]
                        if u[0] == "qk":
                            sw = t2sw[i % 2]
                            ctx.wait(sc, "VS", vs_t1[i])
                            if i >= 2:
                                ctx.wait(sc, "VS", vs_rope[i - 2])
                            _, oc, tsi, off, ln, li = u
                            sc.copy(sw[0:32, 0:ln], p[32:64, 0:ln])
                            sc.copy(sw[32:64, 0:ln], p[0:32, 0:ln])
                            sc.copy(sw[64:96, 0:ln], p[96:128, 0:ln])
                            ctx.inc(sc.copy(sw[96:128, 0:ln], p[64:96, 0:ln]), "SS")
                        else:
                            _, gp, tt, li = u
                            ctx.wait(sc, "TS", ts_mm[i])
                            ctx.inc(sc.activation(v_t[:, tt, gp * 256:(gp + 1) * 256],
                                                  p[:, 0:256], AF.Copy), "SS")
                    elif kind == "ex":
                        j = x
                        ctx.wait(sc, "VS", vs_mx[j])
                        if j >= 2:
                            ctx.wait(sc, "TS", ts_tp[j - 2])
                        sc.activation(pA_bf[j % 2][:], psA[j % 2][:], AF.Exp,
                                      bias=n2A[j % 2][:], scale=consts[:, 3:4],
                                      accum_out=rsA[j % 2][:])
                        ctx.inc(sc.activation(pB_bf[j % 2][:], psB[j % 2][:],
                                              AF.Exp, bias=n2B[j % 2][:],
                                              scale=consts[:, 3:4],
                                              accum_out=rsB[j % 2][:]), "SS")
                    else:
                        emit_oevac(x)

            def pm_sync(s):
                for c in range(16):
                    last_i = c * NT_OWN + (NT_OWN - 1)
                    ctx.wait(s, "SS", ss_oe[last_i])
                    tok = OS.dma(s, d_oscr[c], o_buf[c % 2][:], c % 2)
                    assert tok == ostok[c]

            with ExitStack() as sPM:
                ps_qk = [sPM.enter_context(nc.psum_tensor(f"ps_qk{i}", [128, 512], F32))
                         for i in range(2)]
                psA = [sPM.enter_context(nc.psum_tensor(f"psA{i}", [128, 272], F32))
                       for i in range(2)]
                psB = [sPM.enter_context(nc.psum_tensor(f"psB{i}", [128, 272], F32))
                       for i in range(2)]
                ps_tp = sPM.enter_context(nc.psum_tensor("ps_tp", [128, 768], BF16))
                ps_o = sPM.enter_context(nc.psum_tensor("ps_o", [128, 128], F32))
                block.tensor(pm_tensor)
                block.vector(pm_vector)
                block.scalar(pm_scalar)
                block.sync(pm_sync)
                assert ctx.cnt["TS"] == cTS and ctx.cnt["VS"] == cVS
                assert ctx.cnt["SS"] == cSS
            barrier(ctx, block)
            if _DEBUG:
                def dmp2(s):
                    DBG.dma(s, dbg["qf"], q_f[:].rearrange("p a b -> p (a b)"))
                    DBG.dma(s, dbg["kf"], k_f[:].rearrange("p a b -> p (a b)"))
                    DBG.dma(s, dbg["vt"], v_t[:].rearrange("p a b -> p (a b)"))
                    OS.wait_all(s)
                    DBG.dma(s, dbg["of"], d_oscr)
                    DBG.wait_all(s)
                block.sync(dmp2)
                for emitf in (block.tensor, block.vector, block.scalar, block.gpsimd):
                    def fdbg(eng):
                        DBG.wait_all(eng)
                    emitf(fdbg)
                barrier(ctx, block)
        if _PHASES <= 3:
            return nc

        sB = glob.enter_context(ExitStack())
        g_sb = sB.enter_context(nc.sbuf_tensor("g_sb", [128, 16, TOK], F32))
        with ExitStack() as sC:
            ec = sC.enter_context
            h_ib = ec(nc.sbuf_tensor("h_ib", [128, 8, TOK], BF16))
            wd_ib = ec(nc.sbuf_tensor("wd_ib", [128, 8, 2048], BF16))
            gbuf = [ec(nc.sbuf_tensor(f"gbuf{i}", [128, 2048], BF16)) for i in range(2)]
            ubuf = [ec(nc.sbuf_tensor(f"ubuf{i}", [128, 2048], BF16)) for i in range(2)]
            tg = [ec(nc.sbuf_tensor(f"tg{i}", [128, 512], F32)) for i in range(2)]
            G = Slots(nc, glob, "gg", 2)
            UU = Slots(nc, glob, "uu", 2)
            WD = Slots(nc, glob, "wd", 1)

            base = dict(ctx.cnt)
            gtok = [[None] * 8 for _ in range(8)]
            utok2 = [[None] * 8 for _ in range(8)]
            wdtok = [None] * 8
            ts_gt = {}
            ts_up = {}
            ss_si = {}
            vs_h = {}
            ts_dn = {}
            vs_ge = {}
            cTS, cSS, cVS = base["TS"], base["SS"], base["VS"]
            gc_ = [0, 0]
            uc_ = [0, 0]
            wdc = 0
            for ib in range(8):
                for ic in range(8):
                    gc_[ic % 2] += 16; gtok[ib][ic] = (ic % 2, gc_[ic % 2])
                    uc_[ic % 2] += 16; utok2[ib][ic] = (ic % 2, uc_[ic % 2])
                wdc += 16 * 8; wdtok[ib] = (0, wdc)
                for ic in range(8):
                    for tsx in range(2):
                        cTS += 1; ts_gt[(ib, ic, tsx)] = cTS
                        cSS += 1; ss_si[(ib, ic, tsx)] = cSS
                        cTS += 1; ts_up[(ib, ic, tsx)] = cTS
                        cVS += 1; vs_h[(ib, ic, tsx)] = cVS
                for tsx in range(2):
                    for oc in range(16):
                        cTS += 1; ts_dn[(ib, tsx, oc)] = cTS
                        cVS += 1; vs_ge[(ib, tsx, oc)] = cVS

            ts_keys_g = [(ib, ic, tsx) for ib in range(8) for ic in range(8)
                         for tsx in range(2)]
            dn_keys = [(ib, tsx, oc) for ib in range(8) for tsx in range(2)
                       for oc in range(16)]

            def p5_gp(g):
                OS_outer[0].wait_all(g)
                for ib in range(8):
                    for ic in range(8):
                        pic = ib * 8 + ic - 2
                        if pic >= 0:
                            ctx.wait(g, "TS", ts_gt[(pic // 8, pic % 8, 1)])
                        tok = G.dma(g, gbuf[ic % 2][:], d_wgu[ib * 16 + ic * 2], ic % 2)
                        assert tok == gtok[ib][ic]
                        if pic >= 0:
                            ctx.wait(g, "TS", ts_up[(pic // 8, pic % 8, 1)])
                        tok = UU.dma(g, ubuf[ic % 2][:], d_wgu[ib * 16 + ic * 2 + 1],
                                     ic % 2)
                        assert tok == utok2[ib][ic]
                    if ib >= 1:
                        ctx.wait(g, "TS", ts_dn[(ib - 1, 1, 15)])
                    for ic in range(8):
                        tok = WD.dma(g, wd_ib[:, ic, :], d_wd[ib * 8 + ic], 0)
                    assert tok == wdtok[ib]
            block.gpsimd(p5_gp)

            def p5_tensor(t):
                for ib in range(8):
                    for ic in range(8):
                        for tsx in range(2):
                            gi = (ib * 8 + ic) * 2 + tsx
                            if tsx == 0:
                                G.wait(t, gtok[ib][ic])
                            if gi >= 2:
                                pk = ts_keys_g[gi - 2]
                                ctx.wait(t, "SS", ss_si[pk])
                            pg = ps_g[gi % 2]
                            for kc in range(16):
                                ins = t.matmul(
                                    pg[:], gbuf[ic % 2][:, kc * 128:(kc + 1) * 128],
                                    xn_f[:, kc, HALO + tsx * 512:HALO + (tsx + 1) * 512],
                                    start=(kc == 0), stop=(kc == 15))
                            ctx.inc(ins, "TS")
                            if tsx == 0:
                                UU.wait(t, utok2[ib][ic])
                            if gi >= 2:
                                pk = ts_keys_g[gi - 2]
                                ctx.wait(t, "VS", vs_h[pk])
                            pu = ps_u[gi % 2]
                            for kc in range(16):
                                ins = t.matmul(
                                    pu[:], ubuf[ic % 2][:, kc * 128:(kc + 1) * 128],
                                    xn_f[:, kc, HALO + tsx * 512:HALO + (tsx + 1) * 512],
                                    start=(kc == 0), stop=(kc == 15))
                            ctx.inc(ins, "TS")
                    WD.wait(t, wdtok[ib])
                    for tsx in range(2):
                        for oc in range(16):
                            j = (ib * 2 + tsx) * 16 + oc
                            if j >= 2:
                                ctx.wait(t, "VS", vs_ge[dn_keys[j - 2]])
                            ctx.wait(t, "VS", vs_h[(ib, 7, tsx)])
                            pd = ps_d[j % 2]
                            for ic in range(8):
                                ins = t.matmul(
                                    pd[:], wd_ib[:, ic, oc * 128:(oc + 1) * 128],
                                    h_ib[:, ic, tsx * 512:(tsx + 1) * 512],
                                    start=(ic == 0), stop=(ic == 7))
                            ctx.inc(ins, "TS")

            def p5_scalar(sc):
                for ib in range(8):
                    for ic in range(8):
                        for tsx in range(2):
                            gi = (ib * 8 + ic) * 2 + tsx
                            ctx.wait(sc, "TS", ts_gt[(ib, ic, tsx)])
                            if gi >= 2:
                                pk = ts_keys_g[gi - 2]
                                ctx.wait(sc, "VS", vs_h[pk])
                            ctx.inc(sc.activation(tg[gi % 2][:], ps_g[gi % 2][:],
                                                  AF.Silu), "SS")

            def p5_vector(v):
                for ib in range(8):
                    for ic in range(8):
                        for tsx in range(2):
                            gi = (ib * 8 + ic) * 2 + tsx
                            ctx.wait(v, "TS", ts_up[(ib, ic, tsx)])
                            ctx.wait(v, "SS", ss_si[(ib, ic, tsx)])
                            ctx.inc(v.tensor_mul(h_ib[:, ic, tsx * 512:(tsx + 1) * 512],
                                                 tg[gi % 2][:], ps_u[gi % 2][:]), "VS")
                    for tsx in range(2):
                        for oc in range(16):
                            j = (ib * 2 + tsx) * 16 + oc
                            ctx.wait(v, "TS", ts_dn[(ib, tsx, oc)])
                            dst = g_sb[:, oc, tsx * 512:(tsx + 1) * 512]
                            if ib == 0:
                                ins = v.tensor_copy(dst, ps_d[j % 2][:])
                            else:
                                ins = v.tensor_add(dst, dst, ps_d[j % 2][:])
                            ctx.inc(ins, "VS")

            with ExitStack() as sP5:
                ps_g = [sP5.enter_context(nc.psum_tensor(f"ps_g{i}", [128, 512], F32))
                        for i in range(2)]
                ps_u = [sP5.enter_context(nc.psum_tensor(f"ps_u{i}", [128, 512], F32))
                        for i in range(2)]
                ps_d = [sP5.enter_context(nc.psum_tensor(f"ps_d{i}", [128, 512], F32))
                        for i in range(2)]
                block.tensor(p5_tensor)
                block.scalar(p5_scalar)
                block.vector(p5_vector)
                assert ctx.cnt["TS"] == cTS and ctx.cnt["SS"] == cSS
                assert ctx.cnt["VS"] == cVS
            barrier(ctx, block)

        if _DEBUG:
            def dmp5(s):
                DBG.dma(s, dbg["g"], g_sb[:].rearrange("p a b -> p (a b)"))
                DBG.wait_all(s)
            block.sync(dmp5)
            for emitf in (block.tensor, block.vector, block.scalar, block.gpsimd):
                def fdbg5(eng):
                    DBG.wait_all(eng)
                emitf(fdbg5)
            barrier(ctx, block)
        if _PHASES <= 5:
            return nc

        with ExitStack() as sD:
            ed = sD.enter_context
            o_sb = ed(nc.sbuf_tensor("o_sb", [128, 16, 1024], BF16))
            wobuf = [ed(nc.sbuf_tensor(f"wobuf{i}", [128, 2048], BF16)) for i in range(2)]
            gsil = [ed(nc.sbuf_tensor(f"gsil{i}", [128, 512], F32)) for i in range(2)]
            u2_buf = [ed(nc.sbuf_tensor(f"u2_buf{i}", [128, D], F32)) for i in range(2)]
            out_buf = [ed(nc.sbuf_tensor(f"out_buf{i}", [128, D], F32)) for i in range(2)]
            WO = Slots(nc, glob, "wo", 2)
            OSB = Slots(nc, glob, "osb", 1)
            U2 = Slots(nc, glob, "u2", 2)
            OUT = Slots(nc, glob, "oo", 2)

            base = dict(ctx.cnt)
            cTS, cSS, cVS = base["TS"], base["SS"], base["VS"]
            wotok = [None] * 16
            ts_s = {}
            ss_gs = {}
            vs_sg = {}
            woc = [0, 0]
            for oc in range(16):
                woc[oc % 2] += 16; wotok[oc] = (oc % 2, woc[oc % 2])
                for tsx in range(2):
                    cTS += 1; ts_s[(oc, tsx)] = cTS
                    cSS += 1; ss_gs[(oc, tsx)] = cSS
                    cVS += 1; vs_sg[(oc, tsx)] = cVS
            vs_sg_all = cVS
            u2tok = [None] * NT_OWN
            outtok = [None] * NT_OWN
            ts_tp6 = [[0] * 16 for _ in range(NT_OWN)]
            vs_add = [[0] * 16 for _ in range(NT_OWN)]
            u2c = [0, 0]
            outc = [0, 0]
            for tt in range(NT_OWN):
                u2c[tt % 2] += 16; u2tok[tt] = (tt % 2, u2c[tt % 2])
                for oc in range(16):
                    cTS += 1; ts_tp6[tt][oc] = cTS
                    cVS += 1; vs_add[tt][oc] = cVS
                outc[tt % 2] += 16; outtok[tt] = (tt % 2, outc[tt % 2])

            def p6_sync(s):
                OS.wait_all(s)
                OSB.dma(s, o_sb[:], d_oscr.rearrange("a p b -> p a b"))
                for tt in range(NT_OWN):
                    if tt >= 2:
                        ctx.wait(s, "VS", vs_add[tt - 2][15])
                    tok = U2.dma(s, u2_buf[tt % 2][:],
                                 d_u[HALO + tt * 128:HALO + (tt + 1) * 128, :], tt % 2)
                    assert tok == u2tok[tt]
                    if tt >= 1:
                        ctx.wait(s, "VS", vs_add[tt - 1][15])
                        tok = OUT.dma(s, d_out[(tt - 1) * 128:tt * 128, :],
                                      out_buf[(tt - 1) % 2][:], (tt - 1) % 2)
                        assert tok == outtok[tt - 1]
                ctx.wait(s, "VS", vs_add[NT_OWN - 1][15])
                tok = OUT.dma(s, d_out[(NT_OWN - 1) * 128:NT_OWN * 128, :],
                              out_buf[(NT_OWN - 1) % 2][:], (NT_OWN - 1) % 2)
                assert tok == outtok[NT_OWN - 1]
                OUT.wait_all(s)

            def p6_gp(g):
                for oc in range(16):
                    if oc >= 2:
                        ctx.wait(g, "TS", ts_s[(oc - 2, 1)])
                    tok = WO.dma(g, wobuf[oc % 2][:], d_wo[oc], oc % 2)
                    assert tok == wotok[oc]
            block.gpsimd(p6_gp)

            def p6_tensor(t):
                OSB.wait_all(t)
                for oc in range(16):
                    WO.wait(t, wotok[oc])
                    for tsx in range(2):
                        j = oc * 2 + tsx
                        if j >= 2:
                            ctx.wait(t, "VS", vs_sg[(oc - 1, tsx)])
                        p = ps_s6[j % 2]
                        for kc in range(16):
                            ins = t.matmul(p[:], wobuf[oc % 2][:, kc * 128:(kc + 1) * 128],
                                           o_sb[:, kc, tsx * 512:(tsx + 1) * 512],
                                           start=(kc == 0), stop=(kc == 15))
                        ctx.inc(ins, "TS")
                for tt in range(NT_OWN):
                    for oc in range(16):
                        j6 = tt * 16 + oc
                        if j6 == 0:
                            ctx.wait(t, "VS", vs_sg_all)
                        if j6 >= 2:
                            pj = j6 - 2
                            ctx.wait(t, "VS", vs_add[pj // 16][pj % 16])
                        ctx.inc(t.transpose(ps_tr6[j6 % 2][:],
                                            g_sb[:, oc, tt * 128:(tt + 1) * 128],
                                            id_f[:]), "TS")

            def p6_scalar(sc):
                for oc in range(16):
                    for tsx in range(2):
                        j = oc * 2 + tsx
                        if j >= 2:
                            ctx.wait(sc, "VS", vs_sg[(oc - 1, tsx)])
                        ctx.inc(sc.activation(gsil[j % 2][:],
                                              g_sb[:, oc, tsx * 512:(tsx + 1) * 512],
                                              AF.Silu), "SS")

            def p6_vector(v):
                if _DEBUG:
                    DBG.wait_all(v)
                for oc in range(16):
                    for tsx in range(2):
                        j = oc * 2 + tsx
                        ctx.wait(v, "TS", ts_s[(oc, tsx)])
                        ctx.wait(v, "SS", ss_gs[(oc, tsx)])
                        ctx.inc(v.scalar_tensor_tensor(
                            g_sb[:, oc, tsx * 512:(tsx + 1) * 512],
                            ps_s6[j % 2][:], bo_pc[:, oc:oc + 1], gsil[j % 2][:],
                            OP.add, OP.mult), "VS")
                for tt in range(NT_OWN):
                    U2.wait(v, u2tok[tt])
                    for oc in range(16):
                        ctx.wait(v, "TS", ts_tp6[tt][oc])
                        if tt >= 2 and oc == 0:
                            OUT.wait(v, outtok[tt - 2])
                        ctx.inc(v.tensor_add(out_buf[tt % 2][:, oc * 128:(oc + 1) * 128],
                                             ps_tr6[(tt * 16 + oc) % 2][:],
                                             u2_buf[tt % 2][:, oc * 128:(oc + 1) * 128]),
                                "VS")

            with ExitStack() as sP6:
                ps_s6 = [sP6.enter_context(nc.psum_tensor(f"ps_s6{i}", [128, 512], F32))
                         for i in range(2)]
                ps_tr6 = [sP6.enter_context(nc.psum_tensor(f"ps_tr6{i}", [128, 128], F32))
                          for i in range(2)]
                block.sync(p6_sync)
                block.tensor(p6_tensor)
                block.scalar(p6_scalar)
                block.vector(p6_vector)
                assert ctx.cnt["TS"] == cTS and ctx.cnt["SS"] == cSS
                assert ctx.cnt["VS"] == cVS

            if _DEBUG:
                def dmp6(s):
                    DBG.dma(s, dbg["of"], d_oscr)
                    DBG.wait_all(s)
                block.sync(dmp6)

    return nc

def _prep_inputs(L0_attn_u, L0_attn_Wqkv, L0_attn_bqkv, L0_attn_Wo, L0_attn_bo,
                 L0_attn_sinks, L0_ffn_w_gate_up, L0_ffn_w_down):
    u = np.ascontiguousarray(L0_attn_u, dtype=np.float32)
    Wqkv = np.asarray(L0_attn_Wqkv, dtype=np.float32)
    bqkv = np.asarray(L0_attn_bqkv, dtype=np.float32)
    Wo = np.asarray(L0_attn_Wo, dtype=np.float32)
    bo = np.asarray(L0_attn_bo, dtype=np.float32)
    sinks = np.asarray(L0_attn_sinks, dtype=np.float32)
    Wgu = np.asarray(L0_ffn_w_gate_up, dtype=np.float32)
    Wd = np.asarray(L0_ffn_w_down, dtype=np.float32)

    A = Wqkv[:4096].reshape(32, 128, 16, 128)
    wqkv_pack = np.ascontiguousarray(A.transpose(0, 3, 2, 1)).reshape(32, 128, 2048)
    Bv = Wqkv[4096:].reshape(8, 256, 16, 128)
    v_pack = np.ascontiguousarray(Bv.transpose(0, 3, 2, 1)).reshape(8, 128, 4096)
    Ao = Wo.reshape(16, 128, 16, 128)
    wo_pack = np.ascontiguousarray(Ao.transpose(0, 3, 2, 1)).reshape(16, 128, 2048)
    G = Wgu.reshape(128, 128, 16, 128)
    pack_all = np.ascontiguousarray(G.transpose(0, 3, 2, 1)).reshape(128, 128, 2048)
    order = []
    for ib in range(8):
        for ic in range(8):
            order.append(ib * 8 + ic)
            order.append(64 + ib * 8 + ic)
    wgu_pack = np.ascontiguousarray(pack_all[order])
    wd_pack = np.ascontiguousarray(Wd.T.reshape(64, 128, 2048))

    bqkv_pc = np.ascontiguousarray(bqkv[:4096].reshape(32, 128).T)
    swp = np.concatenate([np.arange(32, 64), np.arange(0, 32),
                          np.arange(96, 128), np.arange(64, 96)])
    bqkv_sw = np.ascontiguousarray(bqkv_pc[swp, :])
    bqkv_pc48 = np.ascontiguousarray(bqkv.reshape(48, 128).T)
    vbias_row = np.ascontiguousarray(bqkv[None, 4096:])
    bo_pc = np.ascontiguousarray(bo.reshape(16, 128).T)
    sinks_t = np.ascontiguousarray(sinks.transpose(0, 2, 1))
    sinks_n = np.ascontiguousarray(sinks)
    ident = np.eye(128, dtype=np.float32)
    ones_row = np.ones((1, EXT), dtype=np.float32)
    consts = np.broadcast_to(np.array([1e-5, -1.0, 1.0 / 2048.0, 0.125], np.float32),
                             (128, 4)).copy()

    inv = (10000.0 ** (-np.arange(32, dtype=np.float64) / 32)).astype(np.float64)

    def rope_tables(pos, scale):
        ang = np.outer(inv, pos.astype(np.float64))
        C = np.tile(np.cos(ang), (4, 1)) * scale
        sn = np.sin(ang)
        S = np.concatenate([-sn, sn, -sn, sn], axis=0) * scale
        return (np.ascontiguousarray(C, dtype=np.float32),
                np.ascontiguousarray(S, dtype=np.float32))

    tabs = []
    masks_q = []
    nb = L // 64
    for q4 in range(4):
        pos_k = np.clip(q4 * TOK - HALO + np.arange(EXT), 0, L - 1)
        ck, sk = rope_tables(pos_k, 1.0)
        m = np.zeros((NT_OWN, 128, 272), np.float32)
        ii = np.arange(128)[:, None]
        jj = np.arange(256)[None, :]
        band = np.where((jj >= ii) & (jj <= ii + 128), 0.0, NEG).astype(np.float32)
        for qt in range(NT_OWN):
            mm = band.copy()
            n0 = q4 * 16 + qt * 2
            if n0 - 1 < 0:
                mm[:, 0:64] = NEG
            if n0 + 2 > nb - 1:
                mm[:, 192:256] = NEG
            m[qt, :, NS:272] = mm
        tabs.append((ck, sk))
        masks_q.append(m)

    in_maps = []
    for c in range(NCORE):
        b = c // 4
        q4 = c % 4
        start = q4 * TOK
        u_ext = np.zeros((EXT, D), np.float32)
        lo = max(0, start - HALO)
        hi = min(L, start + TOK + HALO)
        u_ext[lo - (start - HALO):hi - (start - HALO)] = u[b, lo:hi]
        ck, sk = tabs[q4]
        in_maps.append(dict(
            u_ext=u_ext, wqkv_pack=wqkv_pack, v_pack=v_pack, wo_pack=wo_pack,
            wgu_pack=wgu_pack, wd_pack=wd_pack, bqkv_pc=bqkv_pc48, bqkv_sw=bqkv_sw, bo_pc=bo_pc,
            sinks_t=sinks_t, sinks_n=sinks_n, ck=ck, sk=sk, vbias_row=vbias_row,
            masks=masks_q[q4], ident=ident, ones_row=ones_row, consts=consts,
        ))
    return in_maps


def kernel(L0_attn_u, L0_attn_Wqkv, L0_attn_bqkv, L0_attn_Wo, L0_attn_bo,
           L0_attn_sinks, L0_ffn_w_gate_up, L0_ffn_w_down):
    in_maps = _prep_inputs(L0_attn_u, L0_attn_Wqkv, L0_attn_bqkv, L0_attn_Wo,
                           L0_attn_bo, L0_attn_sinks, L0_ffn_w_gate_up,
                           L0_ffn_w_down)
    nc = build_program2()
    res = run_bass_kernel_spmd(nc, in_maps, core_ids=list(range(NCORE)))
    out = np.empty((B, L, D), np.float32)
    for c in range(NCORE):
        b = c // 4
        q4 = c % 4
        out[b, q4 * TOK:(q4 + 1) * TOK] = res.results[c]["out"]
    return out
